# version 01
# baseline (speedup 1.0000x reference)
import numpy as np
import ml_dtypes
from contextlib import ExitStack

import concourse.bass as bass
import concourse.mybir as mybir
from concourse.bass_utils import run_bass_kernel_spmd

F32 = mybir.dt.float32
BF16 = mybir.dt.bfloat16
AF = mybir.ActivationFunctionType
ALU = mybir.AluOpType
AX = mybir.AxisListType

NCORES = 8
D = 4096
S = 16384
TOK = S // NCORES
TEXT = TOK + 512
T = 512
NT = TOK // T
NH = 32
DH = 128
SCALE = 1.0 / float(np.sqrt(DH))
LN_EPS = 1e-5
RMS_EPS = 1.1920929e-07
TAU_INV = 10.0
SINK_ITERS = 50
NCH = D // 128

ENGS = ("sync", "tensor", "vector", "scalar", "gpsimd")


class Ctr:
    __slots__ = ("h", "n", "key")

    def __init__(self, h, key):
        self.h = h
        self.n = 0
        self.key = key


class B:

    def __init__(self, nc, ctx):
        self.nc = nc
        self.ops = {e: [] for e in ENGS}
        self.waited = {e: {} for e in ENGS}
        self.ctx = ctx
        self._ctrs = {}
        self.sv = self.ctr("sv")
        self.ss = self.ctr("ss")
        self.sp = self.ctr("sp")
        self.dS = self.ctr("dS")
        self.dX = self.ctr("dX")
        self.tok = {}
        self.readers = {}

    def ctr(self, name):
        h = self.ctx.enter_context(self.nc.semaphore(name))
        c = Ctr(h, name)
        self._ctrs[name] = c
        return c

    def emit(self, eng, fn, waits=(), inc=None, k=1):
        wl = []
        seen = self.waited[eng]
        for wtok in waits:
            if wtok is None:
                continue
            c, v = wtok
            if v <= 0:
                continue
            if seen.get(c.key, 0) >= v:
                continue
            seen[c.key] = v
            wl.append((c, v))
        self.ops[eng].append((fn, wl, inc, k))
        if inc is not None:
            inc.n += k
            return (inc, inc.n)
        return None

    def wr(self, name, token):
        self.tok[name] = token
        self.readers[name] = []

    def rd(self, name, token):
        self.readers.setdefault(name, []).append(token)

    def dep_w(self, *names):
        out = []
        for n in names:
            out.append(self.tok.get(n))
            out.extend(self.readers.get(n, ()))
        return out

    def dep_r(self, *names):
        return [self.tok.get(n) for n in names]

    def V(self, fn, deps=()):
        return self.emit("vector", fn, deps, self.sv)

    def Sc(self, fn, deps=()):
        return self.emit("scalar", fn, deps, self.ss)

    def P(self, fn, deps=(), inc=True):
        return self.emit("tensor", fn, deps, self.sp if inc else None)

    def DS(self, fn, deps=()):
        return self.emit("sync", fn, deps, self.dS, 16)

    def DX(self, fn, deps=()):
        return self.emit("gpsimd", fn, deps, self.dX, 16)

    def realize(self, block):
        def mk(eng_name):
            def run(eng):
                for fn, wl, inc, k in self.ops[eng_name]:
                    for c, v in wl:
                        eng.wait_ge(c.h, v)
                    instr = fn(eng)
                    if inc is not None:
                        assert instr is not None
                        instr.then_inc(inc.h, k)
            return run

        block.sync(mk("sync"))
        block.tensor(mk("tensor"))
        block.vector(mk("vector"))
        block.scalar(mk("scalar"))
        block.gpsimd(mk("gpsimd"))


def _ap2(t):
    return t[:, :]


def build_program(n_tiles=NT, debug=False):
    nc = bass.Bass("TRN2", target_bir_lowering=False)

    xT = nc.dram_tensor("xT", [D, TEXT], F32, kind="ExternalInput").ap()
    wq = nc.dram_tensor("wq", [NH, 128, D], F32, kind="ExternalInput").ap()
    wo = nc.dram_tensor("wo", [NH, 128, D], F32, kind="ExternalInput").ap()
    wk = nc.dram_tensor("wk", [128, D], F32, kind="ExternalInput").ap()
    wv = nc.dram_tensor("wv", [128, D], F32, kind="ExternalInput").ap()
    wb = nc.dram_tensor("wb", [128, 8 * 1024], F32, kind="ExternalInput").ap()
    wlog = nc.dram_tensor("wlog", [128, NCH * 24], F32, kind="ExternalInput").ap()
    trigc = nc.dram_tensor("trigc", [128, TEXT], BF16, kind="ExternalInput").ap()
    trigs = nc.dram_tensor("trigs", [128, TEXT], BF16, kind="ExternalInput").ap()
    alph = nc.dram_tensor("alph", [2 * NT, 512], F32, kind="ExternalInput").ap()
    corrb = nc.dram_tensor("corrb", [1, 2 * NT], F32, kind="ExternalInput").ap()
    basesI = nc.dram_tensor("basesI", [24, 1], F32, kind="ExternalInput").ap()
    gateI = nc.dram_tensor("gateI", [1, 1], F32, kind="ExternalInput").ap()
    eyeI = nc.dram_tensor("eyeI", [128, 64], F32, kind="ExternalInput").ap()
    identI = nc.dram_tensor("identI", [128, 128], F32, kind="ExternalInput").ap()

    outT = nc.dram_tensor("outT", [D, TOK], F32, kind="ExternalOutput").ap()

    wq_bf = nc.dram_tensor("wq_bf", [NH, 128, D], BF16, kind="Internal").ap()
    wo_bf = nc.dram_tensor("wo_bf", [NH, 128, D], BF16, kind="Internal").ap()
    wk_bf = nc.dram_tensor("wk_bf", [128, D], BF16, kind="Internal").ap()
    wv_bf = nc.dram_tensor("wv_bf", [128, D], BF16, kind="Internal").ap()
    wb_bf = nc.dram_tensor("wb_bf", [128, 8 * 1024], BF16, kind="Internal").ap()
    wlog_bf = nc.dram_tensor("wlog_bf", [128, NCH * 24], BF16, kind="Internal").ap()

    dbg = {}
    if debug:
        def dtap(name, shape, dtype=F32):
            dbg[name] = nc.dram_tensor(name, shape, dtype, kind="ExternalOutput").ap()
        dtap("d_xn", [128, NCH * T], BF16)
        dtap("d_kr", [128, TEXT], BF16)
        dtap("d_vt", [128, 20 * 128], BF16)
        dtap("d_q0", [128, T], BF16)
        dtap("d_o", [128, NH * T], BF16)
        dtap("d_x1", [128, NCH * T])
        dtap("d_logf", [24, T])
        dtap("d_hfm", [128, 4 * 24])
        dtap("d_xpre", [128, 8 * T], BF16)
        dtap("d_y", [128, 8 * T], BF16)
        dtap("d_ffn", [128, NCH * T], BF16)
        dtap("d_kt", [128, 64])
        dtap("d_as", [128, 16])
        dtap("d_bs", [128, 16])

    ctx = ExitStack()
    with ctx:
        sb = lambda name, shape, dt: ctx.enter_context(nc.sbuf_tensor(name, shape, dt))
        ps = lambda name, shape: ctx.enter_context(nc.psum_tensor(name, shape, F32))

        X = sb("X", [128, NCH * T], F32)
        XN = sb("XN", [128, NCH * T], BF16)
        O = sb("O", [128, NH * T], BF16)
        YB = sb("YB", [128, 8 * T], BF16)
        KR = sb("KR", [128, TEXT], BF16)
        VT = sb("VT", [128, 20 * 128], BF16)
        WS = [sb("WS0", [128, D], BF16), sb("WS1", [128, D], BF16)]
        WLOGS = sb("WLOGS", [128, NCH * 24], BF16)
        TRC = sb("TRC", [128, T], BF16)
        TRS = sb("TRS", [128, T], BF16)
        QT = sb("QT", [128, T], BF16)
        ROTB = sb("ROTB", [128, T], BF16)
        QRs = [sb("QR0", [128, T], BF16), sb("QR1", [128, T], BF16)]
        PRB = sb("PRB", [128, 6 * 256], BF16)
        BCa = sb("BCa", [128, T], F32)
        BCb = sb("BCb", [128, T], F32)
        BCc = sb("BCc", [128, T], F32)
        BCcb = sb("BCcb", [128, T], BF16)
        TMP = sb("TMP", [128, T], F32)
        TMP2 = sb("TMP2", [128, T], F32)
        SQ = sb("SQ", [128, T], BF16)
        XC = sb("XC", [128, T], BF16)
        SM = sb("SM", [1, 6 * 512], F32)
        KT = sb("KT", [128, 64], F32)
        KB2 = sb("KB2", [128, 64], F32)
        AS = sb("AS", [128, 16], F32)
        BS = sb("BS", [128, 16], F32)
        SS2 = sb("SS2", [128, 16], F32)
        HR = sb("HR", [128, 64], F32)
        HPP = sb("HPP", [128, 32], F32)
        HTM = sb("HTM", [128, 4 * 24], F32)
        HFM = sb("HFM", [24, T], F32)
        EYE = sb("EYE", [128, 64], F32)
        IDENT = sb("IDENT", [128, 128], F32)
        ONESF = sb("ONESF", [1, 128], F32)
        ONESB = sb("ONESB", [128, 1], BF16)
        ONESC = sb("ONESC", [128, 1], F32)
        BASES = sb("BASES", [24, 1], F32)
        CORR = sb("CORR", [1, 2 * NT], F32)
        GS = sb("GS", [128, 1], F32)
        GI = sb("GI", [128, 1], F32)
        GATE = sb("GATE", [1, 1], F32)
        ALPH = sb("ALPH", [1, 1024], F32)

        P_MM = ps("P_MM", [128, 512])
        P_S = ps("P_S", [128, 1536])
        P_AV = ps("P_AV", [128, 512])
        P_R1 = ps("P_R1", [1, 512])
        P_R2 = ps("P_R2", [1, 512])
        P_BC = ps("P_BC", [128, 512])

        b = B(nc, ctx)
        block = ctx.enter_context(nc.Block())

        def sm(i, w=512):
            return SM[0:1, i * 512:i * 512 + w]

        castk = b.DX(lambda e: e.dma_start(out=wk_bf, in_=wk))
        castv = b.DX(lambda e: e.dma_start(out=wv_bf, in_=wv))
        castlog = b.DX(lambda e: e.dma_start(out=wlog_bf, in_=wlog))
        castwb = b.DX(lambda e: e.dma_start(out=wb_bf, in_=wb))
        cast_wq = []
        cast_wo = []
        for oc in range(NH):
            cast_wq.append(b.DX(lambda e, oc=oc: e.dma_start(out=wq_bf[oc], in_=wq[oc])))
        for oc in range(NH):
            cast_wo.append(b.DX(lambda e, oc=oc: e.dma_start(out=wo_bf[oc], in_=wo[oc])))

        t = b.DS(lambda e: e.dma_start(out=_ap2(EYE), in_=eyeI)); b.wr("EYE", t)
        t = b.DS(lambda e: e.dma_start(out=_ap2(IDENT), in_=identI)); b.wr("IDENT", t)
        t = b.DS(lambda e: e.dma_start(out=_ap2(BASES), in_=basesI)); b.wr("BASES", t)
        t = b.DS(lambda e: e.dma_start(out=_ap2(CORR), in_=corrb)); b.wr("CORR", t)
        t = b.DS(lambda e: e.dma_start(out=_ap2(GATE), in_=gateI)); b.wr("GATE", t)

        t = b.V(lambda e: e.memset(_ap2(ONESF), 1.0)); b.wr("ONESF", t)
        t = b.V(lambda e: e.memset(_ap2(ONESB), 1.0)); b.wr("ONESB", t)
        t = b.V(lambda e: e.memset(_ap2(ONESC), 1.0)); b.wr("ONESC", t)

        t = b.DS(lambda e: e.dma_start(out=_ap2(WLOGS), in_=wlog_bf), deps=[castlog])
        b.wr("WLOGS", t)

        def ln_stats(src_name, src_f32, nchunks=NCH):
            tokR1 = tokR2 = None
            for dc in range(nchunks):
                tc = b.Sc(lambda e, dc=dc: e.copy(XC[:, :], src_f32(dc)),
                          deps=b.dep_w("XC") + b.dep_r(src_name))
                b.wr("XC", tc)
                tq = b.Sc(lambda e, dc=dc: e.activation(SQ[:, :], src_f32(dc), AF.Square),
                          deps=b.dep_w("SQ"))
                b.wr("SQ", tq)
                t1 = b.P(lambda e, dc=dc: e.matmul(
                    P_R1[:, :], ONESB[:, :], XC[:, :], start=(dc == 0), stop=(dc == nchunks - 1)),
                    deps=[tc] + b.dep_r("ONESB") + (b.dep_w("P_R1") if dc == 0 else []))
                b.rd("XC", t1)
                t2 = b.P(lambda e, dc=dc: e.matmul(
                    P_R2[:, :], ONESB[:, :], SQ[:, :], start=(dc == 0), stop=(dc == nchunks - 1)),
                    deps=[tq] + (b.dep_w("P_R2") if dc == 0 else []))
                b.rd("SQ", t2)
                if dc == nchunks - 1:
                    tokR1, tokR2 = t1, t2
            b.wr("P_R1", tokR1)
            b.wr("P_R2", tokR2)
            return tokR1, tokR2

        def ln_finalize(tokR1, tokR2, want_var_slot=None):
            tm = b.Sc(lambda e: e.mul(sm(3), P_R1[0:1, :], -1.0 / D),
                      deps=[tokR1] + b.dep_w("SM3"))
            b.wr("SM3", tm); b.rd("P_R1", tm)
            te = b.Sc(lambda e: e.mul(sm(4), P_R2[0:1, :], 1.0 / D),
                      deps=[tokR2] + b.dep_w("SM4"))
            b.wr("SM4", te); b.rd("P_R2", te)
            tmsq = b.V(lambda e: e.tensor_mul(sm(5), sm(3), sm(3)),
                       deps=[tm] + b.dep_w("SM5"))
            b.wr("SM5", tmsq)
            tvar = b.V(lambda e: e.tensor_sub(sm(4), sm(4), sm(5)),
                       deps=[te, tmsq])
            b.wr("SM4", tvar); b.rd("SM5", tvar)
            tok_var = None
            if want_var_slot is not None:
                tok_var = b.V(lambda e: e.tensor_copy(sm(want_var_slot), sm(4)),
                              deps=[tvar] + b.dep_w("SM" + str(want_var_slot)))
                b.wr("SM" + str(want_var_slot), tok_var)
                b.rd("SM4", tok_var)
            teps = b.V(lambda e: e.tensor_scalar_add(sm(4), sm(4), LN_EPS), deps=[tvar])
            b.wr("SM4", teps)
            tl = b.Sc(lambda e: e.activation(sm(5), sm(4), AF.Ln),
                      deps=[teps] + b.dep_w("SM5"))
            b.wr("SM5", tl); b.rd("SM4", tl)
            ta = b.Sc(lambda e: e.activation(sm(5), sm(5), AF.Exp, scale=-0.5), deps=[tl])
            b.wr("SM5", ta)
            tb = b.V(lambda e: e.tensor_mul(sm(3), sm(3), sm(5)), deps=[tm, ta])
            b.wr("SM3", tb); b.rd("SM5", tb)
            return ta, tb, tok_var

        def bcast_row(src_ap, src_deps, dst, dst_name, width=T, rows=128, out_bf=None):
            tp = b.P(lambda e: e.matmul(P_BC[0:rows, 0:width], ONESF[0:1, 0:rows], src_ap,
                                        start=True, stop=True),
                     deps=list(src_deps) + b.dep_w("P_BC") + b.dep_r("ONESF"))
            b.wr("P_BC", tp)
            tc = b.Sc(lambda e: e.copy(dst[0:rows, 0:width], P_BC[0:rows, 0:width]),
                      deps=[tp] + b.dep_w(dst_name))
            b.wr(dst_name, tc)
            b.rd("P_BC", tc)
            if out_bf is not None:
                dstb, dstb_name = out_bf
                tb = b.Sc(lambda e: e.copy(dstb[0:rows, 0:width], P_BC[0:rows, 0:width]),
                          deps=[tp] + b.dep_w(dstb_name))
                b.wr(dstb_name, tb)
                b.rd("P_BC", tb)
            return tc

        def ln_apply(dst_name, dst_fn, src_fn, src_name, tok_a, tok_b, resid_fn=None, resid_name=None,
                     nchunks=NCH):
            ba = bcast_row(sm(5), [tok_a], BCa, "BCa")
            bb = bcast_row(sm(3), [tok_b], BCb, "BCb")
            last = None
            for dc in range(nchunks):
                t1 = b.V(lambda e, dc=dc: e.tensor_mul(TMP[:, :], src_fn(dc), BCa[:, :]),
                         deps=b.dep_r(src_name) + [ba] + b.dep_w("TMP"))
                b.wr("TMP", t1); b.rd(src_name, t1); b.rd("BCa", t1)
                if resid_fn is None:
                    t2 = b.V(lambda e, dc=dc: e.tensor_add(dst_fn(dc), TMP[:, :], BCb[:, :]),
                             deps=[t1, bb] + b.dep_w(dst_name))
                else:
                    t1b = b.V(lambda e, dc=dc: e.tensor_add(TMP[:, :], TMP[:, :], BCb[:, :]),
                              deps=[t1, bb])
                    b.wr("TMP", t1b); b.rd("BCb", t1b)
                    t2 = b.V(lambda e, dc=dc: e.tensor_add(dst_fn(dc), resid_fn(dc), TMP[:, :]),
                             deps=[t1b] + b.dep_r(resid_name) + b.dep_w(dst_name))
                    b.rd(resid_name, t2)
                b.wr(dst_name + "#" + str(dc), t2)
                b.rd("TMP", t2); b.rd("BCb", t2)
                last = t2
            b.wr(dst_name, last)
            return last

        tk = b.DS(lambda e: e.dma_start(out=_ap2(WS[0]), in_=wk_bf), deps=[castk])
        b.wr("WS0", tk)
        tv = b.DS(lambda e: e.dma_start(out=_ap2(WS[1]), in_=wv_bf), deps=[castv])
        b.wr("WS1", tv)

        Xv = X[:, :].rearrange("p (dc t) -> p dc t", t=T)
        xTv = xT.rearrange("(dc p) t -> p dc t", p=128)

        def load_x_cols(c0, deps):
            toks = []
            for qq in range(4):
                tt = b.DS(lambda e, qq=qq, c0=c0: e.dma_start(
                    out=Xv[:, qq * 8:(qq + 1) * 8, :],
                    in_=xTv[:, qq * 8:(qq + 1) * 8, c0:c0 + T]), deps=deps)
                toks.append(tt)
            return toks

        def load_trig(c0, deps):
            t1 = b.DS(lambda e, c0=c0: e.dma_start(out=_ap2(TRC), in_=trigc[:, c0:c0 + T]), deps=deps)
            b.wr("TRC", t1)
            t2 = b.DS(lambda e, c0=c0: e.dma_start(out=_ap2(TRS), in_=trigs[:, c0:c0 + T]), deps=deps)
            b.wr("TRS", t2)
            return [t1, t2]

        def rope(dst, dst_name, src, src_name, src_deps):
            c1 = b.Sc(lambda e: e.copy(ROTB[0:64, :], src[64:128, :]),
                      deps=list(src_deps) + b.dep_w("ROTB"))
            c2 = b.Sc(lambda e: e.copy(ROTB[64:128, :], src[0:64, :]),
                      deps=list(src_deps))
            b.wr("ROTB", c2)
            t1 = b.V(lambda e: e.tensor_mul(TMP[:, :], src[:, :], TRC[:, :]),
                     deps=list(src_deps) + b.dep_r("TRC") + b.dep_w("TMP"))
            b.wr("TMP", t1)
            t2 = b.V(lambda e: e.tensor_mul(TMP2[:, :], ROTB[:, :], TRS[:, :]),
                     deps=[c1, c2] + b.dep_r("TRS") + b.dep_w("TMP2"))
            b.wr("TMP2", t2)
            b.rd("ROTB", t2)
            t3 = b.V(lambda e: e.tensor_add(dst[:, :], TMP[:, :], TMP2[:, :]),
                     deps=[t1, t2] + b.dep_w(dst_name))
            b.rd("TMP", t3)
            b.rd("TMP2", t3)
            b.rd(src_name, t3)
            b.rd("TRC", t3)
            b.rd("TRS", t3)
            b.wr(dst_name, t3)
            return t3

        XNv = XN[:, :].rearrange("p (dc t) -> p dc t", t=T)

        for grp in range(5):
            c0 = grp * T
            xl = load_x_cols(c0, b.dep_w("X"))
            b.wr("X", xl[-1])
            ltr = load_trig(c0, b.dep_w("TRC") + b.dep_w("TRS"))
            r1, r2 = ln_stats("X", lambda dc: Xv[:, dc, :])
            ta, tb, _ = ln_finalize(r1, r2)
            ln_apply("XN", lambda dc: XNv[:, dc, :], lambda dc: Xv[:, dc, :], "X", ta, tb)
            mmtok = None
            for dc in range(NCH):
                mmtok = b.P(lambda e, dc=dc: e.matmul(
                    P_MM[:, :], WS[0][:, dc * 128:(dc + 1) * 128], XNv[:, dc, :],
                    start=(dc == 0), stop=(dc == NCH - 1)),
                    deps=b.dep_r("WS0") + b.dep_r("XN") + (b.dep_w("P_MM") if dc == 0 else []))
            b.wr("P_MM", mmtok)
            b.rd("XN", mmtok)
            b.rd("WS0", mmtok)
            tev = b.Sc(lambda e: e.copy(QT[:, :], P_MM[:, :]), deps=[mmtok] + b.dep_w("QT"))
            b.wr("QT", tev)
            b.rd("P_MM", tev)
            rope(KR[:, c0:c0 + T], "KRg" + str(grp), QT[:, :], "QT", [tev])
            b.rd("QT", b.tok["KRg" + str(grp)])
            for jc in range(4):
                vmt = None
                for dc in range(NCH):
                    vmt = b.P(lambda e, dc=dc, jc=jc: e.matmul(
                        P_AV[:, 0:128], XNv[:, dc, :][:, jc * 128:(jc + 1) * 128],
                        WS[1][:, dc * 128:(dc + 1) * 128],
                        start=(dc == 0), stop=(dc == NCH - 1)),
                        deps=b.dep_r("WS1") + b.dep_r("XN") + (b.dep_w("P_AV") if dc == 0 else []))
                b.wr("P_AV", vmt)
                b.rd("XN", vmt)
                b.rd("WS1", vmt)
                jg = grp * 4 + jc
                tve = b.Sc(lambda e, jg=jg: e.copy(VT[:, jg * 128:(jg + 1) * 128], P_AV[:, 0:128]),
                           deps=[vmt])
                b.rd("P_AV", tve)
                b.wr("VTg" + str(jg), tve)
        tok_KR = b.sv, b.sv.n
        tok_VT = b.ss, b.ss.n

        if debug:
            b.DX(lambda e: e.dma_start(out=dbg["d_kr"], in_=KR[:, :]), deps=[tok_KR])
            b.DX(lambda e: e.dma_start(out=dbg["d_vt"], in_=VT[:, :]), deps=[tok_VT])

        Ov = O[:, :].rearrange("p (h t) -> p h t", t=T)

        for tau in range(n_tiles):
            c0 = 256 + tau * T
            xl = load_x_cols(c0, b.dep_w("X"))
            b.wr("X", xl[-1])
            ltr = load_trig(c0, b.dep_w("TRC") + b.dep_w("TRS"))
            tal = b.DS(lambda e, tau=tau: e.dma_start(
                out=_ap2(ALPH), in_=alph[2 * tau:2 * tau + 2, :].rearrange("a w -> (a w)").unsqueeze(0)),
                deps=b.dep_w("ALPH"))
            b.wr("ALPH", tal)

            r1, r2 = ln_stats("X", lambda dc: Xv[:, dc, :])
            ta, tb, _ = ln_finalize(r1, r2)
            ln_apply("XN", lambda dc: XNv[:, dc, :], lambda dc: Xv[:, dc, :], "X", ta, tb)
            o_war = b.dep_w("O")

            if debug and tau == 0:
                tt_ = b.DX(lambda e: e.dma_start(out=dbg["d_xn"], in_=XN[:, :]), deps=b.dep_r("XN"))
                b.rd("XN", tt_)

            def qproj(oc):
                wsb = oc % 2
                wsn = "WS" + str(wsb)
                tl = b.DS(lambda e, oc=oc: e.dma_start(out=_ap2(WS[oc % 2]), in_=wq_bf[oc]),
                          deps=[cast_wq[oc]] + b.dep_w(wsn))
                b.wr(wsn, tl)
                mmt = None
                for dc in range(NCH):
                    mmt = b.P(lambda e, dc=dc, oc=oc: e.matmul(
                        P_MM[:, :], WS[oc % 2][:, dc * 128:(dc + 1) * 128], XNv[:, dc, :],
                        start=(dc == 0), stop=(dc == NCH - 1)),
                        deps=[tl] + b.dep_r("XN") + (b.dep_w("P_MM") if dc == 0 else []))
                b.wr("P_MM", mmt)
                b.rd("XN", mmt)
                b.rd(wsn, mmt)
                tev = b.Sc(lambda e: e.copy(QT[:, :], P_MM[:, :]), deps=[mmt] + b.dep_w("QT"))
                b.wr("QT", tev)
                b.rd("P_MM", tev)
                qrn = "QR" + str(oc % 2)
                rope(QRs[oc % 2][:, :], qrn, QT[:, :], "QT", [tev] + b.dep_w(qrn))
                b.rd("QT", b.tok[qrn])
                if debug and tau == 0 and oc == 0:
                    tt_ = b.DX(lambda e: e.dma_start(out=dbg["d_q0"], in_=QRs[0][:, :]), deps=b.dep_r("QR0"))
                    b.rd("QR0", tt_)

            def attn(oc):
                QR = QRs[oc % 2]
                qrn = "QR" + str(oc % 2)
                for kc in range(2):
                    kidx = 2 * tau + kc
                    jbase = 256 * kidx
                    smt = None
                    for jcl in range(6):
                        smt = b.P(lambda e, jcl=jcl, jbase=jbase, kc=kc: e.matmul(
                            P_S[:, jcl * 256:(jcl + 1) * 256],
                            KR[:, jbase + jcl * 128: jbase + (jcl + 1) * 128],
                            QR[:, kc * 256:(kc + 1) * 256],
                            start=True, stop=True),
                            deps=b.dep_r(qrn) + [tok_KR] + (b.dep_w("P_S") if jcl == 0 else []))
                    b.wr("P_S", smt)
                    b.rd(qrn, smt)
                    tsg = b.Sc(lambda e: e.activation(PRB[:, :], P_S[:, :], AF.Sigmoid, scale=SCALE),
                               deps=[smt] + b.dep_w("PRB"))
                    b.wr("PRB", tsg)
                    b.rd("P_S", tsg)
                    if tau == 0 and oc == 0 and kc == 0:
                        tg = b.Sc(lambda e: e.activation(GATE[:, :], GATE[:, :], AF.Sigmoid),
                                  deps=b.dep_w("GATE"))
                        b.wr("GATE", tg)
                    avt = None
                    for jcl in range(4):
                        avt = b.P(lambda e, jcl=jcl, jbase=jbase: e.matmul(
                            P_AV[:, 0:256],
                            VT[:, (jbase // 128 + jcl) * 128:(jbase // 128 + jcl + 1) * 128],
                            PRB[:, jcl * 256:(jcl + 1) * 256],
                            start=(jcl == 0), stop=(jcl == 3)),
                            deps=[tsg, tok_VT] + (b.dep_w("P_AV") if jcl == 0 else []))
                    for jj, jcl in enumerate(range(2, 6)):
                        avt = b.P(lambda e, jcl=jcl, jbase=jbase: e.matmul(
                            P_AV[:, 256:512],
                            VT[:, (jbase // 128 + jcl) * 128:(jbase // 128 + jcl + 1) * 128],
                            PRB[:, jcl * 256:(jcl + 1) * 256],
                            start=(jcl == 2), stop=(jcl == 5)),
                            deps=[tsg])
                    b.wr("P_AV", avt)
                    sumt = None
                    for jcl in range(4):
                        sumt = b.P(lambda e, jcl=jcl: e.matmul(
                            P_R1[0:1, 0:256], ONESB[:, :], PRB[:, jcl * 256:(jcl + 1) * 256],
                            start=(jcl == 0), stop=(jcl == 3)),
                            deps=[tsg] + b.dep_r("ONESB") + (b.dep_w("P_R1") if jcl == 0 else []))
                    for jcl in range(2, 6):
                        sumt = b.P(lambda e, jcl=jcl: e.matmul(
                            P_R1[0:1, 256:512], ONESB[:, :], PRB[:, jcl * 256:(jcl + 1) * 256],
                            start=(jcl == 2), stop=(jcl == 5)),
                            deps=[tsg])
                    b.wr("P_R1", sumt)
                    b.rd("PRB", sumt)
                    t1 = b.Sc(lambda e: e.copy(sm(0), P_R1[0:1, :]), deps=[sumt] + b.dep_w("SM0"))
                    b.wr("SM0", t1)
                    b.rd("P_R1", t1)
                    t2 = b.V(lambda e, kidx=kidx: e.tensor_scalar(
                        sm(0, 512)[0:1, 256:512], sm(0, 512)[0:1, 256:512],
                        CORR[0:1, kidx:kidx + 1], None, ALU.subtract),
                        deps=[t1] + b.dep_r("CORR"))
                    b.wr("SM0", t2)
                    t3 = b.V(lambda e: e.reciprocal(sm(1), sm(0)), deps=[t2] + b.dep_w("SM1"))
                    b.wr("SM1", t3)
                    b.rd("SM0", t3)
                    t4 = b.V(lambda e, kc=kc: e.tensor_mul(sm(2), ALPH[0:1, kc * 512:(kc + 1) * 512], sm(1)),
                             deps=[t3, tal] + b.dep_w("SM2"))
                    b.wr("SM2", t4)
                    b.rd("SM1", t4)
                    b.rd("ALPH", t4)
                    tbc = bcast_row(sm(2), [t4], BCc, "BCc")
                    b.rd("SM2", tbc)
                    t5 = b.V(lambda e: e.tensor_mul(TMP[:, 0:256], P_AV[:, 0:256], BCc[:, 0:256]),
                             deps=[avt, tbc] + b.dep_w("TMP"))
                    b.wr("TMP", t5)
                    t6 = b.V(lambda e: e.tensor_mul(TMP2[:, 0:256], P_AV[:, 256:512], BCc[:, 256:512]),
                             deps=[avt, tbc] + b.dep_w("TMP2"))
                    b.wr("TMP2", t6)
                    b.rd("P_AV", t6)
                    b.rd("P_AV", t5)
                    b.rd("BCc", t6)
                    b.rd("BCc", t5)
                    t7 = b.V(lambda e, oc=oc, kc=kc: e.tensor_add(
                        Ov[:, oc, :][:, kc * 256:(kc + 1) * 256], TMP[:, 0:256], TMP2[:, 0:256]),
                        deps=[t5, t6] + o_war + b.dep_w("O#%d_%d" % (oc, kc)))
                    b.rd("TMP", t7)
                    b.rd("TMP2", t7)
                    b.wr("O#%d_%d" % (oc, kc), t7)

            qproj(0)
            for oc in range(NH):
                if oc + 1 < NH:
                    qproj(oc + 1)
                attn(oc)
            tok_O = (b.sv, b.sv.n)
            b.wr("O", tok_O)

            if debug and tau == 0:
                tt_ = b.DX(lambda e: e.dma_start(out=dbg["d_o"], in_=O[:, :]), deps=[tok_O])
                b.rd("O", tt_)

            psums = [P_MM, P_AV]
            psnames = ["P_MM", "P_AV"]
            for oc in range(NCH):
                pb = oc % 2
                wsb = oc % 2
                wsn = "WS" + str(wsb)
                tl = b.DS(lambda e, oc=oc: e.dma_start(out=_ap2(WS[oc % 2]), in_=wo_bf[oc]),
                          deps=[cast_wo[oc]] + b.dep_w(wsn))
                b.wr(wsn, tl)
                mmt = None
                for ic in range(NCH):
                    mmt = b.P(lambda e, ic=ic, oc=oc, pb=pb: e.matmul(
                        psums[pb][:, :], WS[oc % 2][:, ic * 128:(ic + 1) * 128], Ov[:, ic, :],
                        start=(ic == 0), stop=(ic == NCH - 1)),
                        deps=[tl, tok_O] + (b.dep_w(psnames[pb]) if ic == 0 else []))
                b.wr(psnames[pb], mmt)
                b.rd(wsn, mmt)
                tev = b.Sc(lambda e, oc=oc, pb=pb: e.copy(XNv[:, oc, :], psums[pb][:, :]),
                           deps=[mmt] + b.dep_w("XN#c" + str(oc)))
                b.rd(psnames[pb], tev)
                b.wr("XN#c" + str(oc), tev)
                tq = b.Sc(lambda e, oc=oc: e.activation(SQ[:, :], XNv[:, oc, :], AF.Square),
                          deps=[tev] + b.dep_w("SQ"))
                b.wr("SQ", tq)
                t1 = b.P(lambda e, oc=oc: e.matmul(
                    P_R1[:, :], ONESB[:, :], XNv[:, oc, :], start=(oc == 0), stop=(oc == NCH - 1)),
                    deps=[tev] + (b.dep_w("P_R1") if oc == 0 else []))
                t2 = b.P(lambda e, oc=oc: e.matmul(
                    P_R2[:, :], ONESB[:, :], SQ[:, :], start=(oc == 0), stop=(oc == NCH - 1)),
                    deps=[tq] + (b.dep_w("P_R2") if oc == 0 else []))
                b.rd("SQ", t2)
                if oc == NCH - 1:
                    b.wr("P_R1", t1)
                    b.wr("P_R2", t2)
                    r1, r2 = t1, t2
            b.wr("XN", (b.ss, b.ss.n))
            ta, tb, _ = ln_finalize(r1, r2)
            ln_apply("X", lambda dc: Xv[:, dc, :], lambda dc: XNv[:, dc, :], "XN", ta, tb,
                     resid_fn=lambda dc: Xv[:, dc, :], resid_name="X")

            if debug and tau == 0:
                tt_ = b.DX(lambda e: e.dma_start(out=dbg["d_x1"], in_=X[:, :]), deps=b.dep_r("X"))
                b.rd("X", tt_)

            r1, r2 = ln_stats("X", lambda dc: Xv[:, dc, :])
            ta, tb, tvar = ln_finalize(r1, r2, want_var_slot=2)
            ln_apply("XN", lambda dc: XNv[:, dc, :], lambda dc: Xv[:, dc, :], "X", ta, tb)

            ts1 = b.V(lambda e: e.tensor_mul(sm(0), sm(5), sm(5)), deps=[ta] + b.dep_w("SM0"))
            b.wr("SM0", ts1)
            b.rd("SM5", ts1)
            ts2 = b.V(lambda e: e.tensor_mul(sm(0), sm(2), sm(0)), deps=[ts1, tvar])
            b.wr("SM0", ts2)
            b.rd("SM2", ts2)
            ts2b = b.V(lambda e: e.tensor_scalar_add(sm(0), sm(0), RMS_EPS), deps=[ts2])
            b.wr("SM0", ts2b)
            ts3 = b.Sc(lambda e: e.activation(sm(0), sm(0), AF.Ln), deps=[ts2b])
            b.wr("SM0", ts3)
            ts4 = b.Sc(lambda e: e.activation(sm(0), sm(0), AF.Exp, scale=-0.5), deps=[ts3])
            b.wr("SM0", ts4)

            lmt = None
            for dc in range(NCH):
                lmt = b.P(lambda e, dc=dc: e.matmul(
                    P_MM[0:24, :], WLOGS[:, dc * 24:(dc + 1) * 24], XNv[:, dc, :],
                    start=(dc == 0), stop=(dc == NCH - 1)),
                    deps=b.dep_r("WLOGS") + b.dep_r("XN") + (b.dep_w("P_MM") if dc == 0 else []))
            b.wr("P_MM", lmt)
            b.rd("XN", lmt)
            b.rd("WLOGS", lmt)
            tbs = bcast_row(sm(0), [ts4], BCc, "BCc", rows=24)
            b.rd("SM0", tbs)
            tlg = b.V(lambda e: e.tensor_mul(HFM[:, :], P_MM[0:24, :], BCc[0:24, :]),
                      deps=[lmt, tbs] + b.dep_w("HFM"))
            b.wr("HFM", tlg)
            b.rd("P_MM", tlg)
            b.rd("BCc", tlg)
            tlg2 = b.V(lambda e: e.tensor_scalar(HFM[:, :], HFM[:, :], BASES[:, :], None, ALU.add),
                       deps=[tlg] + b.dep_r("BASES"))
            b.wr("HFM", tlg2)

            if debug and tau == 0:
                tt_ = b.DX(lambda e: e.dma_start(out=dbg["d_logf"], in_=HFM[:, :]), deps=[tlg2])
                b.rd("HFM", tt_)

            ttr = None
            for g in range(4):
                ttr = b.P(lambda e, g=g: e.transpose(
                    P_BC[:, g * 24:(g + 1) * 24], HFM[:, g * 128:(g + 1) * 128], IDENT[0:24, 0:24]),
                    deps=[tlg2] + b.dep_r("IDENT") + (b.dep_w("P_BC") if g == 0 else []))
            b.wr("P_BC", ttr)
            b.rd("HFM", ttr)
            tz = None
            for g in range(4):
                tz = b.Sc(lambda e, g=g: e.mul(
                    KB2[:, g * 16:(g + 1) * 16], P_BC[:, g * 24:g * 24 + 16], TAU_INV),
                    deps=[ttr] + (b.dep_w("KB2") if g == 0 else []))
            b.wr("KB2", tz)
            b.rd("P_BC", tz)
            tpp = None
            for g in range(4):
                tpp = b.Sc(lambda e, g=g: e.copy(HPP[:, g * 4:(g + 1) * 4], P_BC[:, g * 24 + 16:g * 24 + 20]),
                           deps=[ttr] + (b.dep_w("HPP") if g == 0 else []))
            for g in range(4):
                tpp = b.Sc(lambda e, g=g: e.copy(HPP[:, 16 + g * 4:16 + (g + 1) * 4], P_BC[:, g * 24 + 20:g * 24 + 24]),
                           deps=[ttr])
            b.wr("HPP", tpp)
            b.rd("P_BC", tpp)

            Lv = KB2[:, :].rearrange("p (g r c) -> p g r c", r=4, c=4)
            Ev = KT[:, :].rearrange("p (g r c) -> p g r c", r=4, c=4)
            asv = AS[:, :].rearrange("p (g r) -> p g r", r=4)
            bsv = BS[:, :].rearrange("p (g c) -> p g c", c=4)
            ssv_r = SS2[:, :].rearrange("p (g r) -> p g r", r=4)
            tprev = tz
            for it in range(SINK_ITERS):
                t1 = b.V(lambda e: e.tensor_reduce(SS2[:, :], Lv, AX.X, ALU.max, negate=True),
                         deps=[tprev] + b.dep_w("SS2"))
                b.wr("SS2", t1)
                t2 = b.V(lambda e: e.tensor_tensor(
                    Lv, Lv, ssv_r.unsqueeze(3).broadcast_to([128, 4, 4, 4]), ALU.add),
                    deps=[t1, tprev])
                b.wr("KB2", t2)
                b.rd("SS2", t2)
                t3 = b.Sc(lambda e: e.activation(KT[:, :], KB2[:, :], AF.Exp),
                          deps=[t2] + b.dep_w("KT"))
                b.wr("KT", t3)
                t4 = b.V(lambda e: e.tensor_reduce(SS2[:, :], Ev, AX.X, ALU.add),
                         deps=[t3] + b.dep_w("SS2"))
                b.wr("SS2", t4)
                t5 = b.Sc(lambda e: e.activation(AS[:, :], SS2[:, :], AF.Ln),
                          deps=[t4] + b.dep_w("AS"))
                b.wr("AS", t5)
                b.rd("SS2", t5)
                t6 = b.V(lambda e: e.tensor_sub(
                    Lv, Lv, asv.unsqueeze(3).broadcast_to([128, 4, 4, 4])),
                    deps=[t5, t2])
                b.wr("KB2", t6)
                b.rd("AS", t6)
                t7 = b.V(lambda e: e.tensor_reduce(SS2[:, :], Lv.transpose([0, 1, 3, 2]), AX.X,
                                                   ALU.max, negate=True),
                         deps=[t6] + b.dep_w("SS2"))
                b.wr("SS2", t7)
                t8 = b.V(lambda e: e.tensor_tensor(
                    Lv, Lv, bsv.unsqueeze(2).broadcast_to([128, 4, 4, 4]) if False else
                    SS2[:, :].rearrange("p (g c) -> p g c", c=4).unsqueeze(2).broadcast_to([128, 4, 4, 4]),
                    ALU.add),
                    deps=[t7, t6])
                b.wr("KB2", t8)
                b.rd("SS2", t8)
                t9 = b.Sc(lambda e: e.activation(KT[:, :], KB2[:, :], AF.Exp),
                          deps=[t8] + b.dep_w("KT"))
                b.wr("KT", t9)
                t10 = b.V(lambda e: e.tensor_reduce(SS2[:, :], Ev.transpose([0, 1, 3, 2]), AX.X, ALU.add),
                          deps=[t9] + b.dep_w("SS2"))
                b.wr("SS2", t10)
                t11 = b.Sc(lambda e: e.activation(BS[:, :], SS2[:, :], AF.Ln),
                           deps=[t10] + b.dep_w("BS"))
                b.wr("BS", t11)
                b.rd("SS2", t11)
                t12 = b.V(lambda e: e.tensor_sub(
                    Lv, Lv, BS[:, :].rearrange("p (g c) -> p g c", c=4).unsqueeze(2).broadcast_to([128, 4, 4, 4])),
                    deps=[t11, t8])
                b.wr("KB2", t12)
                b.rd("BS", t12)
                tprev = t12
            tK = b.Sc(lambda e: e.activation(KT[:, :], KB2[:, :], AF.Exp),
                      deps=[tprev] + b.dep_w("KT"))
            b.wr("KT", tK)
            lastA, lastB = tK, tK
            lastA, lastB = tprev, tprev
            th2 = tK
            tg1 = b.P(lambda e: e.matmul(P_BC[:, 0:1], ONESF[0:1, :], GATE[:, :], start=True, stop=True),
                      deps=b.dep_r("GATE") + b.dep_w("P_BC") + b.dep_r("ONESF"))
            b.wr("P_BC", tg1)
            tg2 = b.Sc(lambda e: e.copy(GS[:, :], P_BC[:, 0:1]), deps=[tg1] + b.dep_w("GS"))
            b.wr("GS", tg2)
            b.rd("P_BC", tg2)
            tg3 = b.V(lambda e: e.tensor_scalar(GI[:, :], GS[:, :], -1.0, 1.0, ALU.mult, ALU.add),
                      deps=[tg2] + b.dep_w("GI"))
            b.wr("GI", tg3)
            th3 = b.V(lambda e: e.tensor_scalar(HR[:, :], KT[:, :], GS[:, :], None, ALU.mult),
                      deps=[th2, tg2] + b.dep_w("HR"))
            b.wr("HR", th3)
            b.rd("KT", th3)
            th4 = b.V(lambda e: e.scalar_tensor_tensor(HR[:, :], EYE[:, :], GI[:, :], HR[:, :],
                                                       ALU.mult, ALU.add),
                      deps=[th3, tg3] + b.dep_r("EYE"))
            b.wr("HR", th4)

            tsm1 = b.Sc(lambda e: e.activation(HPP[:, :], HPP[:, :], AF.Exp), deps=[tpp])
            b.wr("HPP", tsm1)
            tsm2 = b.V(lambda e: e.tensor_reduce(SS2[:, 0:8], HPP[:, :].rearrange("p (s c) -> p s c", c=4),
                                                 AX.X, ALU.add),
                       deps=[tsm1] + b.dep_w("SS2"))
            b.wr("SS2", tsm2)
            tsm3 = b.V(lambda e: e.reciprocal(SS2[:, 8:16], SS2[:, 0:8]), deps=[tsm2])
            b.wr("SS2", tsm3)
            tsm4 = b.V(lambda e: e.tensor_tensor(
                HPP[:, :].rearrange("p (s c) -> p s c", c=4), HPP[:, :].rearrange("p (s c) -> p s c", c=4),
                SS2[:, 8:16].unsqueeze(2).broadcast_to([128, 8, 4]), ALU.mult),
                deps=[tsm3, tsm1])
            b.wr("HPP", tsm4)
            b.rd("SS2", tsm4)

            tasm = None
            for g in range(4):
                tasm = b.Sc(lambda e, g=g: e.copy(HTM[:, g * 24:g * 24 + 16], HR[:, g * 16:(g + 1) * 16]),
                            deps=[th4] + (b.dep_w("HTM") if g == 0 else []))
                tasm = b.Sc(lambda e, g=g: e.copy(HTM[:, g * 24 + 16:g * 24 + 20], HPP[:, g * 4:(g + 1) * 4]),
                            deps=[tsm4])
                tasm = b.Sc(lambda e, g=g: e.copy(HTM[:, g * 24 + 20:g * 24 + 24], HPP[:, 16 + g * 4:16 + (g + 1) * 4]),
                            deps=[tsm4])
            b.wr("HTM", tasm)
            b.rd("HR", tasm)
            b.rd("HPP", tasm)
            thf = tasm

            if debug and tau == 0:
                b.DX(lambda e: e.dma_start(out=dbg["d_hfm"], in_=HTM[:, :]), deps=[tasm])

            XPRE = O

            def hrow_bcast(slot, deps):
                ttx = None
                for g in range(4):
                    ttx = b.P(lambda e, g=g, slot=slot: e.transpose(
                        P_MM[0:1, g * 128:(g + 1) * 128],
                        HTM[:, g * 24 + slot:g * 24 + slot + 1], IDENT[:, :]),
                        deps=(list(deps) + b.dep_r("HTM") + b.dep_r("IDENT") + b.dep_w("P_MM"))
                        if g == 0 else [])
                b.wr("P_MM", ttx)
                b.rd("HTM", ttx)
                tev_ = b.Sc(lambda e: e.copy(sm(1), P_MM[0:1, 0:512]),
                            deps=[ttx] + b.dep_w("SM1"))
                b.wr("SM1", tev_)
                b.rd("P_MM", tev_)
                tp_ = b.P(lambda e: e.matmul(P_BC[:, :], ONESF[0:1, :], sm(1),
                                             start=True, stop=True),
                          deps=[tev_] + b.dep_w("P_BC") + b.dep_r("ONESF"))
                b.wr("P_BC", tp_)
                b.rd("SM1", tp_)
                tc_ = b.Sc(lambda e: e.copy(BCcb[:, :], P_BC[:, :]),
                           deps=[tp_] + b.dep_w("BCcb"))
                b.wr("BCcb", tc_)
                b.rd("P_BC", tc_)
                return tc_

            for n in range(4):
                tb_ = hrow_bcast(16 + n, [thf])
                tbb = b.tok["BCcb"]
                for cc in range(8):
                    ch = n * 8 + cc
                    if n == 0:
                        tt = b.V(lambda e, cc=cc, ch=ch: e.tensor_mul(
                            XPRE[:, cc * T:(cc + 1) * T], XNv[:, ch, :], BCcb[:, :]),
                            deps=[tbb] + b.dep_r("XN") + b.dep_w("XP#" + str(cc)))
                    else:
                        t_1 = b.V(lambda e, cc=cc, ch=ch: e.tensor_mul(TMP[:, :], XNv[:, ch, :], BCcb[:, :]),
                                  deps=[tbb] + b.dep_r("XN") + b.dep_w("TMP"))
                        b.wr("TMP", t_1)
                        tt = b.V(lambda e, cc=cc: e.tensor_add(
                            XPRE[:, cc * T:(cc + 1) * T], XPRE[:, cc * T:(cc + 1) * T], TMP[:, :]),
                            deps=[t_1] + b.dep_w("XP#" + str(cc)))
                        b.rd("TMP", tt)
                    b.rd("XN", tt)
                    b.rd("BCcb", tt)
                    b.wr("XP#" + str(cc), tt)
            tok_XPRE = (b.sv, b.sv.n)

            if debug and tau == 0:
                tt_ = b.DX(lambda e: e.dma_start(out=dbg["d_xpre"], in_=O[:, 0:8 * T]), deps=[tok_XPRE])
                for cc_ in range(8):
                    b.rd("XP#" + str(cc_), tt_)

            tw0 = b.DS(lambda e: e.dma_start(out=_ap2(WS[0]), in_=wb_bf[:, 0:4096]),
                       deps=[castwb] + b.dep_w("WS0"))
            b.wr("WS0", tw0)
            tw1 = b.DS(lambda e: e.dma_start(out=_ap2(WS[1]), in_=wb_bf[:, 4096:8192]),
                       deps=[castwb] + b.dep_w("WS1"))
            b.wr("WS1", tw1)
            for co in range(8):
                pb = co % 2
                ymt = None
                for cc in range(8):
                    wsb = WS[cc // 4]
                    ymt = b.P(lambda e, cc=cc, co=co, pb=pb, wsb=wsb: e.matmul(
                        psums[pb][:, :], wsb[:, (cc % 4) * 1024 + co * 128:(cc % 4) * 1024 + (co + 1) * 128],
                        XPRE[:, cc * T:(cc + 1) * T],
                        start=(cc == 0), stop=(cc == 7)),
                        deps=[tw0, tw1, tok_XPRE] + (b.dep_w(psnames[pb]) if cc == 0 else []))
                b.wr(psnames[pb], ymt)
                b.rd("WS0", ymt)
                b.rd("WS1", ymt)
                for cc in range(8):
                    b.rd("XP#" + str(cc), ymt)
                tye = b.Sc(lambda e, co=co, pb=pb: e.copy(YB[:, co * T:(co + 1) * T], psums[pb][:, :]),
                           deps=[ymt] + b.dep_w("YB#" + str(co)))
                b.wr("YB#" + str(co), tye)
                b.rd(psnames[pb], tye)
            tok_YB = (b.ss, b.ss.n)
            b.wr("YB", tok_YB)

            if debug and tau == 0:
                tt_ = b.DX(lambda e: e.dma_start(out=dbg["d_y"], in_=YB[:, :]), deps=[tok_YB])
                b.rd("YB", tt_)

            r1 = r2 = None
            for i in range(4):
                tpost = hrow_bcast(20 + i, [thf])
                tpostb = b.tok["BCcb"]
                for cc in range(8):
                    ch = i * 8 + cc
                    tt = b.V(lambda e, ch=ch, cc=cc: e.tensor_mul(
                        Ov[:, ch, :], YB[:, cc * T:(cc + 1) * T], BCcb[:, :]),
                        deps=[tpostb, tok_YB] + b.dep_w("FO#" + str(ch)) + b.dep_w("XP#" + str(cc)))
                    b.wr("FO#" + str(ch), tt)
                    b.rd("YB", tt)
                    b.rd("BCcb", tt)
                for j in range(4):
                    tj = hrow_bcast(i * 4 + j, [thf])
                    tjb = b.tok["BCcb"]
                    for cc in range(8):
                        ch = i * 8 + cc
                        src_ch = j * 8 + cc
                        t_1 = b.V(lambda e, src_ch=src_ch: e.tensor_mul(TMP[:, :], XNv[:, src_ch, :], BCcb[:, :]),
                                  deps=[tjb] + b.dep_r("XN") + b.dep_w("TMP"))
                        b.wr("TMP", t_1)
                        b.rd("XN", t_1)
                        b.rd("BCcb", t_1)
                        tt = b.V(lambda e, ch=ch: e.tensor_add(Ov[:, ch, :], Ov[:, ch, :], TMP[:, :]),
                                 deps=[t_1, b.tok["FO#" + str(ch)]])
                        b.wr("FO#" + str(ch), tt)
                        b.rd("TMP", tt)
                for cc in range(8):
                    ch = i * 8 + cc
                    tdone = b.tok["FO#" + str(ch)]
                    tq = b.Sc(lambda e, ch=ch: e.activation(SQ[:, :], Ov[:, ch, :], AF.Square),
                              deps=[tdone] + b.dep_w("SQ"))
                    b.wr("SQ", tq)
                    t1s = b.P(lambda e, ch=ch: e.matmul(
                        P_R1[:, :], ONESB[:, :], Ov[:, ch, :], start=(ch == 0), stop=(ch == NCH - 1)),
                        deps=[tdone] + (b.dep_w("P_R1") if ch == 0 else []))
                    b.rd("FO#" + str(ch), t1s)
                    t2s = b.P(lambda e, ch=ch: e.matmul(
                        P_R2[:, :], ONESB[:, :], SQ[:, :], start=(ch == 0), stop=(ch == NCH - 1)),
                        deps=[tq] + (b.dep_w("P_R2") if ch == 0 else []))
                    b.rd("SQ", t2s)
                    if ch == NCH - 1:
                        r1, r2 = t1s, t2s
            b.wr("P_R1", r1)
            b.wr("P_R2", r2)
            tok_FO = (b.sv, b.sv.n)
            b.wr("O", tok_FO)

            if debug and tau == 0:
                tt_ = b.DX(lambda e: e.dma_start(out=dbg["d_ffn"], in_=O[:, :NCH * T]), deps=[tok_FO, r1, r2])
                b.rd("O", tt_)

            ta, tb, _ = ln_finalize(r1, r2)
            ln_apply("X", lambda dc: Xv[:, dc, :], lambda dc: Ov[:, dc, :], "O", ta, tb,
                     resid_fn=lambda dc: Xv[:, dc, :], resid_name="X")
            outv = outT.rearrange("(dc p) t -> p dc t", p=128)
            for qq in range(4):
                ts_ = b.DS(lambda e, qq=qq, tau=tau: e.dma_start(
                    out=outv[:, qq * 8:(qq + 1) * 8, tau * T:tau * T + T],
                    in_=Xv[:, qq * 8:(qq + 1) * 8, :]),
                    deps=b.dep_r("X") + [b.tok.get("X")])
                b.rd("X", ts_)

        b.emit("gpsimd", lambda e: e.nop(), waits=[(b.dS, b.dS.n), (b.dX, b.dX.n)])

        b.realize(block)
    return nc



def _rope_cache_np(Spos, Dh):
    inv_freq = (1.0 / 10000.0 ** (np.arange(0, Dh, 2, dtype=np.float32) / Dh)).astype(np.float32)
    t = np.arange(Spos, dtype=np.float32)
    f = np.outer(t, inv_freq).astype(np.float32)
    cos = np.zeros((Spos, Dh), dtype=np.float32)
    sin = np.zeros((Spos, Dh), dtype=np.float32)
    cos[:, 0::2] = np.cos(f)
    cos[:, 1::2] = np.cos(f)
    sin[:, 0::2] = np.sin(f)
    sin[:, 1::2] = np.sin(f)
    return cos, sin


def prep_inputs(L0_attn_x, L0_attn_Wq, L0_attn_Wk, L0_attn_Wv, L0_attn_Wo,
                L0_ffn_W_res, L0_ffn_W_pre, L0_ffn_W_post, L0_ffn_H_res_base,
                L0_ffn_H_pre_base, L0_ffn_H_post_base, L0_ffn_gate, L0_ffn_W_branch):
    x = np.asarray(L0_attn_x, dtype=np.float32)[0]
    Wq = np.asarray(L0_attn_Wq, dtype=np.float32)
    Wk = np.asarray(L0_attn_Wk, dtype=np.float32)
    Wv = np.asarray(L0_attn_Wv, dtype=np.float32)
    Wo = np.asarray(L0_attn_Wo, dtype=np.float32)
    Wres = np.asarray(L0_ffn_W_res, dtype=np.float32)
    Wpre = np.asarray(L0_ffn_W_pre, dtype=np.float32)
    Wpost = np.asarray(L0_ffn_W_post, dtype=np.float32)
    Hres_b = np.asarray(L0_ffn_H_res_base, dtype=np.float32)
    Hpre_b = np.asarray(L0_ffn_H_pre_base, dtype=np.float32)
    Hpost_b = np.asarray(L0_ffn_H_post_base, dtype=np.float32)
    gate = np.asarray(L0_ffn_gate, dtype=np.float32)
    Wb = np.asarray(L0_ffn_W_branch, dtype=np.float32)

    wq_h = np.ascontiguousarray(Wq.reshape(NH, 128, NCH, 128).transpose(0, 3, 2, 1)).reshape(NH, 128, D)
    wo_h = np.ascontiguousarray(Wo.reshape(NH, 128, NCH, 128).transpose(0, 3, 2, 1)).reshape(NH, 128, D)
    wk_h = np.ascontiguousarray(Wk.T.reshape(NCH, 128, 128).transpose(1, 0, 2)).reshape(128, D)
    wv_h = np.ascontiguousarray(Wv.T.reshape(NCH, 128, 128).transpose(1, 0, 2)).reshape(128, D)
    wb_h = np.ascontiguousarray(Wb.reshape(8, 128, 1024).transpose(1, 0, 2)).reshape(128, 8 * 1024)
    Wcat = np.concatenate([Wres, Wpre, Wpost], axis=0)
    wlog_h = np.ascontiguousarray(Wcat.T.reshape(NCH, 128, 24).transpose(1, 0, 2)).reshape(128, NCH * 24)
    bases = np.concatenate([Hres_b.reshape(-1), Hpre_b.reshape(-1), Hpost_b.reshape(-1)]).reshape(24, 1)
    bases = np.ascontiguousarray(bases, dtype=np.float32)
    gate_h = gate.reshape(1, 1).astype(np.float32)
    eye_h = np.ascontiguousarray(np.broadcast_to(np.tile(np.eye(4, dtype=np.float32).reshape(-1), 4), (128, 64)))
    ident_h = np.eye(128, dtype=np.float32)

    cos_full, sin_full = _rope_cache_np(S, DH)
    lin = np.linspace(0.0, 1.0, 256, dtype=np.float32)

    in_maps = []
    for c in range(NCORES):
        lo = 2048 * c - 256
        hi = 2048 * c + 2304
        xTc = np.zeros((D, TEXT), dtype=np.float32)
        a = max(lo, 0)
        bnd = min(hi, S)
        xTc[:, a - lo:bnd - lo] = x[a:bnd].T
        pos = np.clip(np.arange(lo, hi), 0, S - 1)
        trigc_h = np.ascontiguousarray(cos_full[pos].T).astype(ml_dtypes.bfloat16)
        sgn = sin_full[pos].T.copy()
        sgn[0:64, :] *= -1.0
        trigs_h = np.ascontiguousarray(sgn).astype(ml_dtypes.bfloat16)
        alph_h = np.zeros((2 * NT, 512), dtype=np.float32)
        for k in range(2 * NT):
            if c == 0 and k == 0:
                alph_h[k, 0:256] = 0.0
                alph_h[k, 256:512] = 1.0
            else:
                alph_h[k, 0:256] = 1.0 - lin
                alph_h[k, 256:512] = lin
        corr_h = np.zeros((1, 2 * NT), dtype=np.float32)
        if c == NCORES - 1:
            corr_h[0, 2 * NT - 1] = 128.0
        in_maps.append({
            "xT": np.ascontiguousarray(xTc),
            "wq": wq_h, "wo": wo_h, "wk": wk_h, "wv": wv_h, "wb": wb_h, "wlog": wlog_h,
            "trigc": trigc_h, "trigs": trigs_h, "alph": alph_h, "corrb": corr_h,
            "basesI": bases, "gateI": gate_h, "eyeI": eye_h, "identI": ident_h,
        })
    return in_maps


def kernel(**inputs):
    in_maps = prep_inputs(**inputs)
    nc = build_program()
    res = run_bass_kernel_spmd(nc, in_maps, core_ids=list(range(NCORES)))
    out = np.empty((1, S, D), dtype=np.float32)
    for c in range(NCORES):
        out[0, 2048 * c:2048 * (c + 1), :] = res.results[c]["outT"].T
    return out


# version 06
# speedup vs baseline: 1.5482x; 1.5482x over previous
import numpy as np
import ml_dtypes
from contextlib import ExitStack

import concourse.bass as bass
import concourse.mybir as mybir
from concourse.bass_utils import run_bass_kernel_spmd

F32 = mybir.dt.float32
BF16 = mybir.dt.bfloat16
AF = mybir.ActivationFunctionType
ALU = mybir.AluOpType
AX = mybir.AxisListType

NCORES = 8
D = 4096
S = 16384
TOK = S // NCORES
TEXT = TOK + 512
T = 512
NT = TOK // T
NH = 32
DH = 128
SCALE = 1.0 / float(np.sqrt(DH))
LN_EPS = 1e-5
RMS_EPS = 1.1920929e-07
TAU_INV = 10.0
SINK_ITERS = 50
NCH = D // 128

ENGS = ("sync", "tensor", "vector", "scalar", "gpsimd")


class Ctr:
    __slots__ = ("h", "n", "key")

    def __init__(self, h, key):
        self.h = h
        self.n = 0
        self.key = key


class B:

    def __init__(self, nc, ctx):
        self.nc = nc
        self.ops = {e: [] for e in ENGS}
        self.waited = {e: {} for e in ENGS}
        self.ctx = ctx
        self._ctrs = {}
        self.sv = self.ctr("sv")
        self.ss = self.ctr("ss")
        self.sp = self.ctr("sp")
        self.dS = self.ctr("dS")
        self.dX = self.ctr("dX")
        self.tok = {}
        self.readers = {}

    def ctr(self, name):
        h = self.ctx.enter_context(self.nc.semaphore(name))
        c = Ctr(h, name)
        self._ctrs[name] = c
        return c

    def emit(self, eng, fn, waits=(), inc=None, k=1):
        wl = []
        seen = self.waited[eng]
        for wtok in waits:
            if wtok is None:
                continue
            c, v = wtok
            if v <= 0:
                continue
            if seen.get(c.key, 0) >= v:
                continue
            seen[c.key] = v
            wl.append((c, v))
        self.ops[eng].append((fn, wl, inc, k))
        if inc is not None:
            inc.n += k
            return (inc, inc.n)
        return None

    def wr(self, name, token):
        self.tok[name] = token
        self.readers[name] = []

    def rd(self, name, token):
        self.readers.setdefault(name, []).append(token)

    def dep_w(self, *names):
        out = []
        for n in names:
            out.append(self.tok.get(n))
            out.extend(self.readers.get(n, ()))
        return out

    def dep_r(self, *names):
        return [self.tok.get(n) for n in names]

    def V(self, fn, deps=()):
        return self.emit("vector", fn, deps, self.sv)

    def Sc(self, fn, deps=()):
        return self.emit("scalar", fn, deps, self.ss)

    def P(self, fn, deps=(), inc=True):
        return self.emit("tensor", fn, deps, self.sp if inc else None)

    def DS(self, fn, deps=()):
        return self.emit("sync", fn, deps, self.dS, 16)

    def DX(self, fn, deps=()):
        return self.emit("gpsimd", fn, deps, self.dX, 16)

    def realize(self, block):
        def mk(eng_name):
            def run(eng):
                for fn, wl, inc, k in self.ops[eng_name]:
                    for c, v in wl:
                        eng.wait_ge(c.h, v)
                    instr = fn(eng)
                    if inc is not None:
                        assert instr is not None
                        instr.then_inc(inc.h, k)
            return run

        block.sync(mk("sync"))
        block.tensor(mk("tensor"))
        block.vector(mk("vector"))
        block.scalar(mk("scalar"))
        block.gpsimd(mk("gpsimd"))


def _ap2(t):
    return t[:, :]


def build_program(n_tiles=NT, debug=False):
    nc = bass.Bass("TRN2", target_bir_lowering=False, dynamic_dma_scratch_size=8192)

    xT = nc.dram_tensor("xT", [D, TEXT], F32, kind="ExternalInput").ap()
    wq = nc.dram_tensor("wq", [NH, 128, D], F32, kind="ExternalInput").ap()
    wo = nc.dram_tensor("wo", [NH, 128, D], F32, kind="ExternalInput").ap()
    wk = nc.dram_tensor("wk", [128, D], F32, kind="ExternalInput").ap()
    wv = nc.dram_tensor("wv", [128, D], F32, kind="ExternalInput").ap()
    wb = nc.dram_tensor("wb", [128, 8 * 1024], F32, kind="ExternalInput").ap()
    wlog = nc.dram_tensor("wlog", [128, NCH * 24], F32, kind="ExternalInput").ap()
    trigc = nc.dram_tensor("trigc", [128, TEXT], BF16, kind="ExternalInput").ap()
    trigs = nc.dram_tensor("trigs", [128, TEXT], BF16, kind="ExternalInput").ap()
    alph = nc.dram_tensor("alph", [2 * NT, 512], F32, kind="ExternalInput").ap()
    corrb = nc.dram_tensor("corrb", [1, 2 * NT], F32, kind="ExternalInput").ap()
    basesI = nc.dram_tensor("basesI", [24, 1], F32, kind="ExternalInput").ap()
    gateI = nc.dram_tensor("gateI", [1, 1], F32, kind="ExternalInput").ap()
    eyeI = nc.dram_tensor("eyeI", [128, 64], F32, kind="ExternalInput").ap()
    identI = nc.dram_tensor("identI", [128, 128], F32, kind="ExternalInput").ap()

    outT = nc.dram_tensor("outT", [D, TOK], F32, kind="ExternalOutput").ap()

    wq_bf = nc.dram_tensor("wq_bf", [NH, 128, D], BF16, kind="Internal").ap()
    wo_bf = nc.dram_tensor("wo_bf", [NH, 128, D], BF16, kind="Internal").ap()
    wk_bf = nc.dram_tensor("wk_bf", [128, D], BF16, kind="Internal").ap()
    wv_bf = nc.dram_tensor("wv_bf", [128, D], BF16, kind="Internal").ap()
    wb_bf = nc.dram_tensor("wb_bf", [128, 8 * 1024], BF16, kind="Internal").ap()
    wlog_bf = nc.dram_tensor("wlog_bf", [128, NCH * 24], BF16, kind="Internal").ap()

    dbg = {}
    if debug:
        def dtap(name, shape, dtype=F32):
            dbg[name] = nc.dram_tensor(name, shape, dtype, kind="ExternalOutput").ap()
        dtap("d_xn", [128, NCH * T], BF16)
        dtap("d_kr", [128, TEXT], BF16)
        dtap("d_vt", [128, 20 * 128], BF16)
        dtap("d_q0", [128, T], BF16)
        dtap("d_o", [128, NH * T], BF16)
        dtap("d_x1", [128, NCH * T])
        dtap("d_logf", [24, T])
        dtap("d_hfm", [16, T], BF16)
        dtap("d_xpre", [128, 8 * T], BF16)
        dtap("d_y", [128, 8 * T], BF16)
        dtap("d_ffn", [128, NCH * T], BF16)
        dtap("d_kt", [128, 64])
        dtap("d_hpp", [8, T], BF16)
        dtap("d_bcc", [128, T], BF16)
        dtap("d_pbc", [128, T])
        dtap("d_as", [128, 16])
        dtap("d_bs", [128, 16])

    ctx = ExitStack()
    with ctx:
        sb = lambda name, shape, dt: ctx.enter_context(nc.sbuf_tensor(name, shape, dt))
        ps = lambda name, shape: ctx.enter_context(nc.psum_tensor(name, shape, F32))

        X = sb("X", [128, NCH * T], F32)
        XN = sb("XN", [128, NCH * T], BF16)
        O = sb("O", [128, NH * T], BF16)
        YB = sb("YB", [128, 8 * T], BF16)
        KR = sb("KR", [128, TEXT], BF16)
        VT = sb("VT", [128, 20 * 128], BF16)
        WS = [sb("WS0", [128, D], BF16), sb("WS1", [128, D], BF16)]
        WLOGS = sb("WLOGS", [128, NCH * 24], BF16)
        TRC = sb("TRC", [128, T], BF16)
        TRS = sb("TRS", [128, T], BF16)
        QT = sb("QT", [128, T], BF16)
        ROTB = sb("ROTB", [128, T], BF16)
        QRs = [sb("QR0", [128, T], BF16), sb("QR1", [128, T], BF16)]
        PRB = sb("PRB", [128, 6 * 256], BF16)
        BCb = sb("BCb", [128, T], F32)
        BCab = sb("BCab", [128, T], BF16)
        BCc = sb("BCc", [128, T], F32)
        BCcb = sb("BCcb", [128, T], BF16)
        TMP = sb("TMP", [128, T], F32)
        SQ2b = [sb("SQa", [128, T], BF16), sb("SQb", [128, T], BF16)]
        XC2b = [sb("XCa", [128, T], BF16), sb("XCb", [128, T], BF16)]
        TM8 = sb("TM8", [128, 8 * T], BF16)
        SM = sb("SM", [32, 5 * 512], F32)
        KT = sb("KT", [128, 64], F32)
        KB2 = sb("KB2", [128, 64], F32)
        AS = sb("AS", [128, 16], F32)
        BS = sb("BS", [128, 16], F32)
        SS2 = sb("SS2", [128, 16], F32)
        HR = sb("HR", [128, 64], F32)
        HPP = sb("HPP", [128, 32], F32)
        HPP8 = sb("HPP8", [128, 8], F32)
        HFMR = sb("HFMR", [32, T], F32)
        HFMPP = sb("HFMPP", [32, T], F32)
        IDENTB = sb("IDENTB", [128, 128], BF16)
        EYE = sb("EYE", [128, 64], F32)
        IDENT = sb("IDENT", [128, 128], F32)
        ONESF = sb("ONESF", [1, 128], F32)
        ONESB = sb("ONESB", [128, 1], BF16)
        BASES = sb("BASES", [24, 1], F32)
        CORR = sb("CORR", [1, 2 * NT], F32)
        GS = sb("GS", [128, 1], F32)
        GI = sb("GI", [128, 1], F32)
        GATE = sb("GATE", [32, 1], F32)
        ALPH = sb("ALPH", [1, 1024], F32)

        P_MM = ps("P_MM", [128, 512])
        P_S = ps("P_S", [128, 1536])
        P_AV = ps("P_AV", [128, 512])
        P_R1 = ps("P_R1", [1, 512])
        P_R2 = ps("P_R2", [1, 512])
        P_BC = ps("P_BC", [128, 512])

        HFM = TM8[:, :].bitcast(F32)[0:24, 0:T]

        b = B(nc, ctx)
        block = ctx.enter_context(nc.Block())

        def sm(i, w=512):
            return SM[0:1, i * 512:i * 512 + w]

        castk = b.DX(lambda e: e.dma_start(out=wk_bf, in_=wk))
        castv = b.DX(lambda e: e.dma_start(out=wv_bf, in_=wv))
        castlog = b.DX(lambda e: e.dma_start(out=wlog_bf, in_=wlog))
        castwb = b.DX(lambda e: e.dma_start(out=wb_bf, in_=wb))
        cast_wq = []
        cast_wo = []
        for oc in range(NH):
            cast_wq.append(b.DX(lambda e, oc=oc: e.dma_start(out=wq_bf[oc], in_=wq[oc])))
        for oc in range(NH):
            cast_wo.append(b.DX(lambda e, oc=oc: e.dma_start(out=wo_bf[oc], in_=wo[oc])))

        t = b.DS(lambda e: e.dma_start(out=_ap2(EYE), in_=eyeI)); b.wr("EYE", t)
        t = b.DS(lambda e: e.dma_start(out=_ap2(IDENT), in_=identI)); b.wr("IDENT", t)
        t = b.DS(lambda e: e.dma_start(out=_ap2(BASES), in_=basesI)); b.wr("BASES", t)
        t = b.DS(lambda e: e.dma_start(out=_ap2(CORR), in_=corrb)); b.wr("CORR", t)
        t = b.DS(lambda e: e.dma_start(out=GATE[0:1, 0:1], in_=gateI)); b.wr("GATE", t)

        t = b.V(lambda e: e.tensor_copy(_ap2(IDENTB), _ap2(IDENT)), deps=b.dep_r("IDENT")); b.wr("IDENTB", t)
        t = b.V(lambda e: e.memset(_ap2(ONESF), 1.0)); b.wr("ONESF", t)
        t = b.V(lambda e: e.memset(_ap2(ONESB), 1.0)); b.wr("ONESB", t)

        t = b.DS(lambda e: e.dma_start(out=_ap2(WLOGS), in_=wlog_bf), deps=[castlog])
        b.wr("WLOGS", t)

        def ln_stats(src_name, src_f32, nchunks=NCH):
            tokR1 = tokR2 = None
            for dc in range(nchunks):
                pb = dc % 2
                XC = XC2b[pb]; SQ = SQ2b[pb]
                xcn = "XC" + str(pb); sqn = "SQ" + str(pb)
                tc = b.Sc(lambda e, dc=dc, XC=XC: e.copy(XC[:, :], src_f32(dc)),
                          deps=b.dep_w(xcn) + b.dep_r(src_name))
                b.wr(xcn, tc)
                tq = b.Sc(lambda e, dc=dc, SQ=SQ: e.activation(SQ[:, :], src_f32(dc), AF.Square),
                          deps=b.dep_w(sqn))
                b.wr(sqn, tq)
                t1 = b.P(lambda e, dc=dc, XC=XC: e.matmul(
                    P_R1[:, :], ONESB[:, :], XC[:, :], start=(dc == 0), stop=(dc == nchunks - 1)),
                    deps=[tc] + b.dep_r("ONESB") + (b.dep_w("P_R1") if dc == 0 else []))
                b.rd(xcn, t1)
                t2 = b.P(lambda e, dc=dc, SQ=SQ: e.matmul(
                    P_R2[:, :], ONESB[:, :], SQ[:, :], start=(dc == 0), stop=(dc == nchunks - 1)),
                    deps=[tq] + (b.dep_w("P_R2") if dc == 0 else []))
                b.rd(sqn, t2)
                if dc == nchunks - 1:
                    tokR1, tokR2 = t1, t2
            b.wr("P_R1", tokR1)
            b.wr("P_R2", tokR2)
            return tokR1, tokR2

        def ln_finalize(tokR1, tokR2, want_var_slot=None):
            tm = b.Sc(lambda e: e.mul(sm(3), P_R1[0:1, :], -1.0 / D),
                      deps=[tokR1] + b.dep_w("SM3"))
            b.wr("SM3", tm); b.rd("P_R1", tm)
            te = b.Sc(lambda e: e.mul(sm(4), P_R2[0:1, :], 1.0 / D),
                      deps=[tokR2] + b.dep_w("SM4"))
            b.wr("SM4", te); b.rd("P_R2", te)
            tmsq = b.V(lambda e: e.tensor_mul(sm(2), sm(3), sm(3)),
                       deps=[tm] + b.dep_w("SM2"))
            b.wr("SM2", tmsq)
            tvar = b.V(lambda e: e.tensor_sub(sm(4), sm(4), sm(2)),
                       deps=[te, tmsq])
            b.wr("SM4", tvar); b.rd("SM2", tvar)
            tok_var = None
            if want_var_slot is not None:
                tok_var = b.V(lambda e: e.tensor_copy(sm(want_var_slot), sm(4)),
                              deps=[tvar] + b.dep_w("SM" + str(want_var_slot)))
                b.wr("SM" + str(want_var_slot), tok_var)
                b.rd("SM4", tok_var)
            teps = b.V(lambda e: e.tensor_scalar_add(sm(4), sm(4), LN_EPS), deps=[tvar])
            b.wr("SM4", teps)
            tl = b.Sc(lambda e: e.activation(sm(2), sm(4), AF.Ln),
                      deps=[teps] + b.dep_w("SM2"))
            b.wr("SM2", tl); b.rd("SM4", tl)
            ta = b.Sc(lambda e: e.activation(sm(2), sm(2), AF.Exp, scale=-0.5), deps=[tl])
            b.wr("SM2", ta)
            tb = b.V(lambda e: e.tensor_mul(sm(3), sm(3), sm(2)), deps=[tm, ta])
            b.wr("SM3", tb); b.rd("SM2", tb)
            return ta, tb, tok_var

        def bcast_row(src_slot, src_deps, dst, dst_name, width=T, rows=128, out_bf=None):
            tp = b.P(lambda e: e.matmul(P_BC[0:rows, 0:width],
                                        IDENT[0:32, 0:1].broadcast_to([32, rows]),
                                        SM[0:32, src_slot * 512:src_slot * 512 + width],
                                        start=True, stop=True),
                     deps=list(src_deps) + b.dep_w("P_BC") + b.dep_r("IDENT"))
            b.wr("P_BC", tp)
            tc = b.Sc(lambda e: e.copy(dst[0:rows, 0:width], P_BC[0:rows, 0:width]),
                      deps=[tp] + b.dep_w(dst_name))
            b.wr(dst_name, tc)
            b.rd("P_BC", tc)
            if out_bf is not None:
                dstb, dstb_name = out_bf
                tb = b.Sc(lambda e: e.copy(dstb[0:rows, 0:width], P_BC[0:rows, 0:width]),
                          deps=[tp] + b.dep_w(dstb_name))
                b.wr(dstb_name, tb)
                b.rd("P_BC", tb)
            return tc

        def ln_apply(dst_name, dst_all, src_all, src_name, tok_a, tok_b, resid=False):
            ba = bcast_row(2, [tok_a], BCab, "BCab")
            bb = bcast_row(3, [tok_b], BCb, "BCb")
            dv = lambda ap: ap.rearrange("p (dc t) -> p dc t", t=T)
            if not resid:
                t1 = b.V(lambda e: e.tensor_tensor(
                    dv(dst_all), dv(src_all), BCab[:, :].unsqueeze(1).broadcast_to([128, NCH, T]),
                    ALU.mult),
                    deps=b.dep_r(src_name) + [ba] + b.dep_w(dst_name))
                b.wr(dst_name, t1)
                b.rd(src_name, t1)
                b.rd("BCab", t1)
                t2 = b.V(lambda e: e.tensor_tensor(
                    dv(dst_all), dv(dst_all), BCb[:, :].unsqueeze(1).broadcast_to([128, NCH, T]),
                    ALU.add),
                    deps=[t1, bb])
                b.wr(dst_name, t2)
                b.rd("BCb", t2)
                return t2
            else:
                t1 = b.V(lambda e: e.tensor_tensor(
                    dv(src_all), dv(src_all), BCab[:, :].unsqueeze(1).broadcast_to([128, NCH, T]),
                    ALU.mult),
                    deps=b.dep_w(src_name) + [ba])
                b.wr(src_name, t1)
                b.rd("BCab", t1)
                t2 = b.V(lambda e: e.tensor_add(dv(dst_all), dv(dst_all), dv(src_all)),
                         deps=[t1] + b.dep_w(dst_name))
                b.wr(dst_name, t2)
                b.rd(src_name, t2)
                t3 = b.V(lambda e: e.tensor_tensor(
                    dv(dst_all), dv(dst_all), BCb[:, :].unsqueeze(1).broadcast_to([128, NCH, T]),
                    ALU.add),
                    deps=[t2, bb])
                b.wr(dst_name, t3)
                b.rd("BCb", t3)
                return t3

        tk = b.DS(lambda e: e.dma_start(out=_ap2(WS[0]), in_=wk_bf), deps=[castk])
        b.wr("WS0", tk)
        tv = b.DS(lambda e: e.dma_start(out=_ap2(WS[1]), in_=wv_bf), deps=[castv])
        b.wr("WS1", tv)

        Xv = X[:, :].rearrange("p (dc t) -> p dc t", t=T)
        xTv = xT.rearrange("(dc p) t -> p dc t", p=128)

        def load_x_cols(c0, deps):
            toks = []
            for qq in range(4):
                tt = b.DS(lambda e, qq=qq, c0=c0: e.dma_start(
                    out=Xv[:, qq * 8:(qq + 1) * 8, :],
                    in_=xTv[:, qq * 8:(qq + 1) * 8, c0:c0 + T]), deps=deps)
                toks.append(tt)
            return toks

        def load_trig(c0, deps):
            t1 = b.DS(lambda e, c0=c0: e.dma_start(out=_ap2(TRC), in_=trigc[:, c0:c0 + T]), deps=deps)
            b.wr("TRC", t1)
            t2 = b.DS(lambda e, c0=c0: e.dma_start(out=_ap2(TRS), in_=trigs[:, c0:c0 + T]), deps=deps)
            b.wr("TRS", t2)
            return [t1, t2]

        def rope(dst, dst_name, src, src_name, src_deps):
            c1 = b.Sc(lambda e: e.copy(ROTB[0:64, :], src[64:128, :]),
                      deps=list(src_deps) + b.dep_w("ROTB"))
            c2 = b.Sc(lambda e: e.copy(ROTB[64:128, :], src[0:64, :]),
                      deps=list(src_deps))
            b.wr("ROTB", c2)
            t1 = b.V(lambda e: e.tensor_mul(dst[:, :], src[:, :], TRC[:, :]),
                     deps=list(src_deps) + b.dep_r("TRC") + b.dep_w(dst_name))
            t2 = b.V(lambda e: e.tensor_mul(TMP[:, :], ROTB[:, :], TRS[:, :]),
                     deps=[c1, c2] + b.dep_r("TRS") + b.dep_w("TMP"))
            b.wr("TMP", t2)
            b.rd("ROTB", t2)
            t3 = b.V(lambda e: e.tensor_add(dst[:, :], dst[:, :], TMP[:, :]),
                     deps=[t1, t2])
            b.rd("TMP", t3)
            b.rd(src_name, t3)
            b.rd("TRC", t3)
            b.rd("TRS", t3)
            b.wr(dst_name, t3)
            return t3

        XNv = XN[:, :].rearrange("p (dc t) -> p dc t", t=T)

        for grp in range(5):
            c0 = grp * T
            xl = load_x_cols(c0, b.dep_w("X"))
            b.wr("X", xl[-1])
            ltr = load_trig(c0, b.dep_w("TRC") + b.dep_w("TRS"))
            r1, r2 = ln_stats("X", lambda dc: Xv[:, dc, :])
            ta, tb, _ = ln_finalize(r1, r2)
            ln_apply("XN", XN[:, :], X[:, :], "X", ta, tb)
            mmtok = None
            for dc in range(NCH):
                mmtok = b.P(lambda e, dc=dc: e.matmul(
                    P_MM[:, :], WS[0][:, dc * 128:(dc + 1) * 128], XNv[:, dc, :],
                    start=(dc == 0), stop=(dc == NCH - 1)),
                    deps=b.dep_r("WS0") + b.dep_r("XN") + (b.dep_w("P_MM") if dc == 0 else []))
            b.wr("P_MM", mmtok)
            b.rd("XN", mmtok)
            b.rd("WS0", mmtok)
            tev = b.Sc(lambda e: e.copy(QT[:, :], P_MM[:, :]), deps=[mmtok] + b.dep_w("QT"))
            b.wr("QT", tev)
            b.rd("P_MM", tev)
            rope(KR[:, c0:c0 + T], "KRg" + str(grp), QT[:, :], "QT", [tev])
            b.rd("QT", b.tok["KRg" + str(grp)])
            for jc in range(4):
                vmt = None
                for dc in range(NCH):
                    vmt = b.P(lambda e, dc=dc, jc=jc: e.matmul(
                        P_AV[:, 0:128], XNv[:, dc, :][:, jc * 128:(jc + 1) * 128],
                        WS[1][:, dc * 128:(dc + 1) * 128],
                        start=(dc == 0), stop=(dc == NCH - 1)),
                        deps=b.dep_r("WS1") + b.dep_r("XN") + (b.dep_w("P_AV") if dc == 0 else []))
                b.wr("P_AV", vmt)
                b.rd("XN", vmt)
                b.rd("WS1", vmt)
                jg = grp * 4 + jc
                tve = b.Sc(lambda e, jg=jg: e.copy(VT[:, jg * 128:(jg + 1) * 128], P_AV[:, 0:128]),
                           deps=[vmt])
                b.rd("P_AV", tve)
                b.wr("VTg" + str(jg), tve)
        tok_KR = b.sv, b.sv.n
        tok_VT = b.ss, b.ss.n

        if debug:
            b.DX(lambda e: e.dma_start(out=dbg["d_kr"], in_=KR[:, :]), deps=[tok_KR])
            b.DX(lambda e: e.dma_start(out=dbg["d_vt"], in_=VT[:, :]), deps=[tok_VT])

        Ov = O[:, :].rearrange("p (h t) -> p h t", t=T)

        for tau in range(n_tiles):
            c0 = 256 + tau * T
            xl = load_x_cols(c0, b.dep_w("X"))
            b.wr("X", xl[-1])
            ltr = load_trig(c0, b.dep_w("TRC") + b.dep_w("TRS"))
            tal = b.DS(lambda e, tau=tau: e.dma_start(
                out=_ap2(ALPH), in_=alph[2 * tau:2 * tau + 2, :].rearrange("a w -> (a w)").unsqueeze(0)),
                deps=b.dep_w("ALPH"))
            b.wr("ALPH", tal)

            r1, r2 = ln_stats("X", lambda dc: Xv[:, dc, :])
            ta, tb, _ = ln_finalize(r1, r2)
            ln_apply("XN", XN[:, :], X[:, :], "X", ta, tb)
            o_war = b.dep_w("O")

            if debug and tau == 0:
                tt_ = b.DX(lambda e: e.dma_start(out=dbg["d_xn"], in_=XN[:, :]), deps=b.dep_r("XN"))
                b.rd("XN", tt_)

            def qproj(oc):
                wsb = oc % 2
                wsn = "WS" + str(wsb)
                tl = b.DS(lambda e, oc=oc: e.dma_start(out=_ap2(WS[oc % 2]), in_=wq_bf[oc]),
                          deps=[cast_wq[oc]] + b.dep_w(wsn))
                b.wr(wsn, tl)
                mmt = None
                for dc in range(NCH):
                    mmt = b.P(lambda e, dc=dc, oc=oc: e.matmul(
                        P_MM[:, :], WS[oc % 2][:, dc * 128:(dc + 1) * 128], XNv[:, dc, :],
                        start=(dc == 0), stop=(dc == NCH - 1)),
                        deps=[tl] + b.dep_r("XN") + (b.dep_w("P_MM") if dc == 0 else []))
                b.wr("P_MM", mmt)
                b.rd("XN", mmt)
                b.rd(wsn, mmt)
                tev = b.Sc(lambda e: e.copy(QT[:, :], P_MM[:, :]), deps=[mmt] + b.dep_w("QT"))
                b.wr("QT", tev)
                b.rd("P_MM", tev)
                qrn = "QR" + str(oc % 2)
                rope(QRs[oc % 2][:, :], qrn, QT[:, :], "QT", [tev] + b.dep_w(qrn))
                b.rd("QT", b.tok[qrn])
                if debug and tau == 0 and oc == 0:
                    tt_ = b.DX(lambda e: e.dma_start(out=dbg["d_q0"], in_=QRs[0][:, :]), deps=b.dep_r("QR0"))
                    b.rd("QR0", tt_)

            def attn(oc):
                QR = QRs[oc % 2]
                qrn = "QR" + str(oc % 2)
                av_meta = []
                for kc in range(2):
                    kidx = 2 * tau + kc
                    jbase = 256 * kidx
                    smt = None
                    for jcl in range(6):
                        smt = b.P(lambda e, jcl=jcl, jbase=jbase, kc=kc: e.matmul(
                            P_S[:, jcl * 256:(jcl + 1) * 256],
                            KR[:, jbase + jcl * 128: jbase + (jcl + 1) * 128],
                            QR[:, kc * 256:(kc + 1) * 256],
                            start=True, stop=True),
                            deps=b.dep_r(qrn) + [tok_KR] + (b.dep_w("P_S") if jcl == 0 else []))
                    b.wr("P_S", smt)
                    b.rd(qrn, smt)
                    tsg = b.Sc(lambda e: e.activation(PRB[:, :], P_S[:, :], AF.Sigmoid, scale=SCALE),
                               deps=[smt] + b.dep_w("PRB"))
                    b.wr("PRB", tsg)
                    b.rd("P_S", tsg)
                    if tau == 0 and oc == 0 and kc == 0:
                        tg = b.Sc(lambda e: e.activation(GATE[0:1, 0:1], GATE[0:1, 0:1], AF.Sigmoid),
                                  deps=b.dep_w("GATE"))
                        b.wr("GATE", tg)
                    if kc == 0:
                        avdst = P_AV[:, 0:512]
                        avname = "P_AV"
                    else:
                        avdst = P_S[:, 1024:1536]
                        avname = "P_S"
                    avt = None
                    for jcl in range(4):
                        avt = b.P(lambda e, jcl=jcl, jbase=jbase, avdst=avdst: e.matmul(
                            avdst[:, 0:256],
                            VT[:, (jbase // 128 + jcl) * 128:(jbase // 128 + jcl + 1) * 128],
                            PRB[:, jcl * 256:(jcl + 1) * 256],
                            start=(jcl == 0), stop=(jcl == 3)),
                            deps=[tsg, tok_VT] + (b.dep_w(avname) if (kc == 0 and jcl == 0) else []))
                    for jcl in range(2, 6):
                        avt = b.P(lambda e, jcl=jcl, jbase=jbase, avdst=avdst: e.matmul(
                            avdst[:, 256:512],
                            VT[:, (jbase // 128 + jcl) * 128:(jbase // 128 + jcl + 1) * 128],
                            PRB[:, jcl * 256:(jcl + 1) * 256],
                            start=(jcl == 2), stop=(jcl == 5)),
                            deps=[tsg])
                    if kc == 0:
                        b.wr("P_AV", avt)
                    else:
                        b.wr("P_S", avt)
                    psum_r = P_R1 if kc == 0 else P_R2
                    prname = "P_R1" if kc == 0 else "P_R2"
                    sumt = None
                    for jcl in range(4):
                        sumt = b.P(lambda e, jcl=jcl, psum_r=psum_r: e.matmul(
                            psum_r[0:1, 0:256], ONESB[:, :], PRB[:, jcl * 256:(jcl + 1) * 256],
                            start=(jcl == 0), stop=(jcl == 3)),
                            deps=[tsg] + b.dep_r("ONESB") + (b.dep_w(prname) if jcl == 0 else []))
                    for jcl in range(2, 6):
                        sumt = b.P(lambda e, jcl=jcl, psum_r=psum_r: e.matmul(
                            psum_r[0:1, 256:512], ONESB[:, :], PRB[:, jcl * 256:(jcl + 1) * 256],
                            start=(jcl == 2), stop=(jcl == 5)),
                            deps=[tsg])
                    b.wr(prname, sumt)
                    b.rd("PRB", sumt)
                    s_cp, s_rc = (0, 1) if kc == 0 else (3, 4)
                    t1 = b.Sc(lambda e, psum_r=psum_r, s_cp=s_cp: e.copy(sm(s_cp), psum_r[0:1, :]),
                              deps=[sumt] + b.dep_w("SM" + str(s_cp)))
                    b.wr("SM" + str(s_cp), t1)
                    b.rd(prname, t1)
                    t2 = b.V(lambda e, kidx=kidx, s_cp=s_cp: e.tensor_scalar(
                        sm(s_cp, 512)[0:1, 256:512], sm(s_cp, 512)[0:1, 256:512],
                        CORR[0:1, kidx:kidx + 1], None, ALU.subtract),
                        deps=[t1] + b.dep_r("CORR"))
                    b.wr("SM" + str(s_cp), t2)
                    t3 = b.V(lambda e, s_cp=s_cp, s_rc=s_rc: e.reciprocal(sm(s_rc), sm(s_cp)),
                             deps=[t2] + b.dep_w("SM" + str(s_rc)))
                    b.wr("SM" + str(s_rc), t3)
                    b.rd("SM" + str(s_cp), t3)
                    t4 = b.V(lambda e, kc=kc, s_rc=s_rc: e.tensor_mul(
                        sm(s_rc), ALPH[0:1, kc * 512:(kc + 1) * 512], sm(s_rc)),
                        deps=[t3, tal])
                    b.wr("SM" + str(s_rc), t4)
                    b.rd("ALPH", t4)
                    av_meta.append((avdst, avname, avt, s_rc, t4))
                for kc in range(2):
                    avdst, avname, avt, s_rc, t4 = av_meta[kc]
                    cbuf, cname = (BCc, "BCc") if kc == 0 else (BCb, "BCb")
                    tpb = b.P(lambda e, s_rc=s_rc: e.matmul(
                        P_BC[:, :], IDENT[0:32, 0:1].broadcast_to([32, 128]),
                        SM[0:32, s_rc * 512:s_rc * 512 + 512], start=True, stop=True),
                        deps=[t4] + b.dep_r("IDENT") + b.dep_w("P_BC"))
                    b.wr("P_BC", tpb)
                    b.rd("SM" + str(s_rc), tpb)
                    tbc = b.Sc(lambda e, cbuf=cbuf: e.copy(cbuf[:, :], P_BC[:, :]),
                               deps=[tpb] + b.dep_w(cname))
                    b.wr(cname, tbc)
                    b.rd("P_BC", tbc)
                    t5 = b.V(lambda e, avdst=avdst, cbuf=cbuf: e.tensor_mul(TMP[:, :], avdst[:, :], cbuf[:, :]),
                             deps=[avt, tbc] + b.dep_w("TMP"))
                    b.wr("TMP", t5)
                    b.rd(avname, t5)
                    b.rd(cname, t5)
                    t7 = b.V(lambda e, oc=oc, kc=kc: e.tensor_add(
                        Ov[:, oc, :][:, kc * 256:(kc + 1) * 256], TMP[:, 0:256], TMP[:, 256:512]),
                        deps=[t5] + o_war + b.dep_w("O#%d_%d" % (oc, kc)))
                    b.rd("TMP", t7)
                    b.wr("O#%d_%d" % (oc, kc), t7)

            qproj(0)
            for oc in range(NH):
                if oc + 1 < NH:
                    qproj(oc + 1)
                attn(oc)
            tok_O = (b.sv, b.sv.n)
            b.wr("O", tok_O)

            if debug and tau == 0:
                tt_ = b.DX(lambda e: e.dma_start(out=dbg["d_o"], in_=O[:, :]), deps=[tok_O])
                b.rd("O", tt_)

            psums = [P_MM, P_AV]
            psnames = ["P_MM", "P_AV"]
            for oc in range(NCH):
                pb = oc % 2
                wsb = oc % 2
                wsn = "WS" + str(wsb)
                tl = b.DS(lambda e, oc=oc: e.dma_start(out=_ap2(WS[oc % 2]), in_=wo_bf[oc]),
                          deps=[cast_wo[oc]] + b.dep_w(wsn))
                b.wr(wsn, tl)
                mmt = None
                for ic in range(NCH):
                    mmt = b.P(lambda e, ic=ic, oc=oc, pb=pb: e.matmul(
                        psums[pb][:, :], WS[oc % 2][:, ic * 128:(ic + 1) * 128], Ov[:, ic, :],
                        start=(ic == 0), stop=(ic == NCH - 1)),
                        deps=[tl, tok_O] + (b.dep_w(psnames[pb]) if ic == 0 else []))
                b.wr(psnames[pb], mmt)
                b.rd(wsn, mmt)
                tev = b.Sc(lambda e, oc=oc, pb=pb: e.copy(XNv[:, oc, :], psums[pb][:, :]),
                           deps=[mmt] + b.dep_w("XN#c" + str(oc)))
                b.rd(psnames[pb], tev)
                b.wr("XN#c" + str(oc), tev)
                tq = b.Sc(lambda e, oc=oc: e.activation(SQ[:, :], XNv[:, oc, :], AF.Square),
                          deps=[tev] + b.dep_w("SQ"))
                b.wr("SQ", tq)
                t1 = b.P(lambda e, oc=oc: e.matmul(
                    P_R1[:, :], ONESB[:, :], XNv[:, oc, :], start=(oc == 0), stop=(oc == NCH - 1)),
                    deps=[tev] + (b.dep_w("P_R1") if oc == 0 else []))
                t2 = b.P(lambda e, oc=oc: e.matmul(
                    P_R2[:, :], ONESB[:, :], SQ[:, :], start=(oc == 0), stop=(oc == NCH - 1)),
                    deps=[tq] + (b.dep_w("P_R2") if oc == 0 else []))
                b.rd("SQ", t2)
                if oc == NCH - 1:
                    b.wr("P_R1", t1)
                    b.wr("P_R2", t2)
                    r1, r2 = t1, t2
            b.wr("XN", (b.ss, b.ss.n))
            ta, tb, _ = ln_finalize(r1, r2)
            ln_apply("X", X[:, :], XN[:, :], "XN", ta, tb, resid=True)

            if debug and tau == 0:
                tt_ = b.DX(lambda e: e.dma_start(out=dbg["d_x1"], in_=X[:, :]), deps=b.dep_r("X"))
                b.rd("X", tt_)

            r1, r2 = ln_stats("X", lambda dc: Xv[:, dc, :])
            ta, tb, tvar = ln_finalize(r1, r2, want_var_slot=1)
            ln_apply("XN", XN[:, :], X[:, :], "X", ta, tb)

            ts1 = b.V(lambda e: e.tensor_mul(sm(0), sm(2), sm(2)), deps=[ta] + b.dep_w("SM0"))
            b.wr("SM0", ts1)
            b.rd("SM2", ts1)
            ts2 = b.V(lambda e: e.tensor_mul(sm(0), sm(1), sm(0)), deps=[ts1, tvar])
            b.wr("SM0", ts2)
            b.rd("SM1", ts2)
            ts2b = b.V(lambda e: e.tensor_scalar_add(sm(0), sm(0), RMS_EPS), deps=[ts2])
            b.wr("SM0", ts2b)
            ts3 = b.Sc(lambda e: e.activation(sm(0), sm(0), AF.Ln), deps=[ts2b])
            b.wr("SM0", ts3)
            ts4 = b.Sc(lambda e: e.activation(sm(0), sm(0), AF.Exp, scale=-0.5), deps=[ts3])
            b.wr("SM0", ts4)

            lmt = None
            for dc in range(NCH):
                lmt = b.P(lambda e, dc=dc: e.matmul(
                    P_MM[0:24, :], WLOGS[:, dc * 24:(dc + 1) * 24], XNv[:, dc, :],
                    start=(dc == 0), stop=(dc == NCH - 1)),
                    deps=b.dep_r("WLOGS") + b.dep_r("XN") + (b.dep_w("P_MM") if dc == 0 else []))
            b.wr("P_MM", lmt)
            b.rd("XN", lmt)
            b.rd("WLOGS", lmt)
            tbs = bcast_row(0, [ts4], BCc, "BCc", rows=24)
            b.rd("SM0", tbs)
            tlg = b.V(lambda e: e.tensor_mul(HFM[:, :], P_MM[0:24, :], BCc[0:24, :]),
                      deps=[lmt, tbs] + b.dep_w("HFM"))
            b.wr("HFM", tlg)
            b.rd("P_MM", tlg)
            b.rd("BCc", tlg)
            tlg2 = b.V(lambda e: e.tensor_scalar(HFM[:, :], HFM[:, :], BASES[:, :], None, ALU.add),
                       deps=[tlg] + b.dep_r("BASES"))
            b.wr("HFM", tlg2)

            if debug and tau == 0:
                tt_ = b.DX(lambda e: e.dma_start(out=dbg["d_logf"], in_=HFM[:, :]), deps=[tlg2])
                b.rd("HFM", tt_)

            ttr = None
            for g in range(4):
                ttr = b.P(lambda e, g=g: e.transpose(
                    P_BC[:, g * 24:(g + 1) * 24], HFM[:, g * 128:(g + 1) * 128], IDENT[0:24, 0:24]),
                    deps=[tlg2] + b.dep_r("IDENT") + (b.dep_w("P_BC") if g == 0 else []))
            b.wr("P_BC", ttr)
            b.rd("HFM", ttr)
            tz = None
            for g in range(4):
                tz = b.Sc(lambda e, g=g: e.mul(
                    KB2[:, g * 16:(g + 1) * 16], P_BC[:, g * 24:g * 24 + 16], TAU_INV),
                    deps=[ttr] + (b.dep_w("KB2") if g == 0 else []))
            b.wr("KB2", tz)
            b.rd("P_BC", tz)
            tpp = None
            for g in range(4):
                tpp = b.Sc(lambda e, g=g: e.copy(HPP[:, g * 4:(g + 1) * 4], P_BC[:, g * 24 + 16:g * 24 + 20]),
                           deps=[ttr] + (b.dep_w("HPP") if g == 0 else []))
            for g in range(4):
                tpp = b.Sc(lambda e, g=g: e.copy(HPP[:, 16 + g * 4:16 + (g + 1) * 4], P_BC[:, g * 24 + 20:g * 24 + 24]),
                           deps=[ttr])
            b.wr("HPP", tpp)
            b.rd("P_BC", tpp)

            tsm1 = b.Sc(lambda e: e.activation(HPP[:, :], HPP[:, :], AF.Exp), deps=[tpp])
            b.wr("HPP", tsm1)
            tsm2 = b.V(lambda e: e.tensor_reduce(SS2[:, 0:8], HPP[:, :].rearrange("p (s c) -> p s c", c=4),
                                                 AX.X, ALU.add),
                       deps=[tsm1] + b.dep_w("SS2"))
            b.wr("SS2", tsm2)
            tsm3 = b.V(lambda e: e.reciprocal(SS2[:, 8:16], SS2[:, 0:8]), deps=[tsm2])
            b.wr("SS2", tsm3)
            tsm4 = b.V(lambda e: e.tensor_tensor(
                HPP[:, :].rearrange("p (s c) -> p s c", c=4), HPP[:, :].rearrange("p (s c) -> p s c", c=4),
                SS2[:, 8:16].unsqueeze(2).broadcast_to([128, 8, 4]), ALU.mult),
                deps=[tsm3, tsm1])
            b.wr("HPP", tsm4)
            b.rd("SS2", tsm4)


            HPPg = HPP[:, :].rearrange("p (s g n) -> p s g n", s=2, n=4)
            ttp = None
            for g in range(4):
                tcp = b.Sc(lambda e, g=g: e.copy(HPP8[:, :].rearrange("p (s n) -> p s n", n=4),
                                                 HPPg[:, :, g, :]),
                           deps=[tsm4] + b.dep_w("HPP8"))
                b.wr("HPP8", tcp)
                ttp = b.P(lambda e, g=g: e.transpose(
                    P_MM[0:8, g * 128:(g + 1) * 128], HPP8[:, :], IDENT[:, :]),
                    deps=([tcp] + b.dep_r("IDENT") + b.dep_w("P_MM")) if g == 0 else [tcp])
                b.rd("HPP8", ttp)
            b.wr("P_MM", ttp)
            b.rd("HPP", ttp)
            tfp = b.V(lambda e: e.tensor_copy(HFMPP[0:8, :], P_MM[0:8, 0:T]),
                      deps=[ttp] + b.dep_w("HFMPP"))
            b.wr("HFMPP", tfp)
            b.rd("P_MM", tfp)

            BCBUF = [(BCcb, "BCcb"), (BCab, "BCab")]

            def sel_bcast(srct, src_name, slot, deps, buf):
                dstt, dstn = BCBUF[buf]
                td_ = b.DS(lambda e, slot=slot, srct=srct: e.dma_start(
                    out=SM[0:1, 0:T], in_=srct[slot:slot + 1, :]),
                    deps=list(deps) + b.dep_w("SM0"))
                b.wr("SM0", td_)
                b.rd(src_name, td_)
                tp_ = b.P(lambda e: e.matmul(P_BC[:, :], IDENT[0:32, 0:1].broadcast_to([32, 128]),
                                             SM[0:32, 0:T], start=True, stop=True),
                          deps=[td_] + b.dep_r("IDENT") + b.dep_w("P_BC"))
                b.wr("P_BC", tp_)
                b.rd("SM0", tp_)
                tc_ = b.Sc(lambda e, dstt=dstt: e.copy(dstt[:, :], P_BC[:, :]),
                           deps=[tp_] + b.dep_w(dstn))
                b.wr(dstn, tc_)
                b.rd("P_BC", tc_)
                return tc_

            XP8 = O[:, 0:8 * T].rearrange("p (cc t) -> p cc t", t=T)
            XN8 = lambda n: XN[:, n * 8 * T:(n + 1) * 8 * T].rearrange("p (cc t) -> p cc t", t=T)
            TM8v = TM8[:, :].rearrange("p (cc t) -> p cc t", t=T)

            def bc8(buf):
                return BCBUF[buf][0][:, :].unsqueeze(1).broadcast_to([128, 8, T])

            tbb = [None, None]
            tbb[0] = sel_bcast(HFMPP, "HFMPP", 0, [tfp], 0)
            for n in range(4):
                buf = n % 2
                if n + 1 < 4:
                    tbb[(n + 1) % 2] = sel_bcast(HFMPP, "HFMPP", n + 1, [tfp], (n + 1) % 2)
                dstn = BCBUF[buf][1]
                if n == 0:
                    tt = b.V(lambda e, buf=buf: e.tensor_tensor(XP8, XN8(0), bc8(buf), ALU.mult),
                             deps=[tbb[buf]] + b.dep_r("XN") + b.dep_w("XP"))
                else:
                    t_1 = b.V(lambda e, n=n, buf=buf: e.tensor_tensor(TM8v, XN8(n), bc8(buf), ALU.mult),
                              deps=[tbb[buf]] + b.dep_r("XN") + b.dep_w("TM8"))
                    b.wr("TM8", t_1)
                    b.rd("XN", t_1)
                    b.rd(dstn, t_1)
                    tt = b.V(lambda e: e.tensor_tensor(XP8, XP8, TM8v, ALU.add),
                             deps=[t_1] + b.dep_w("XP"))
                    b.rd("TM8", tt)
                b.rd("XN", tt)
                b.rd(dstn, tt)
                b.wr("XP", tt)
            tok_XPRE = (b.sv, b.sv.n)

            if debug and tau == 0:
                tt_ = b.DX(lambda e: e.dma_start(out=dbg["d_xpre"], in_=O[:, 0:8 * T]), deps=[tok_XPRE])
                b.rd("XP", tt_)

            XPRE = O
            tw0 = b.DS(lambda e: e.dma_start(out=_ap2(WS[0]), in_=wb_bf[:, 0:4096]),
                       deps=[castwb] + b.dep_w("WS0"))
            b.wr("WS0", tw0)
            tw1 = b.DS(lambda e: e.dma_start(out=_ap2(WS[1]), in_=wb_bf[:, 4096:8192]),
                       deps=[castwb] + b.dep_w("WS1"))
            b.wr("WS1", tw1)
            for co in range(8):
                pb = co % 2
                ymt = None
                for cc in range(8):
                    wsb = WS[cc // 4]
                    ymt = b.P(lambda e, cc=cc, co=co, pb=pb, wsb=wsb: e.matmul(
                        psums[pb][:, :], wsb[:, (cc % 4) * 1024 + co * 128:(cc % 4) * 1024 + (co + 1) * 128],
                        XPRE[:, cc * T:(cc + 1) * T],
                        start=(cc == 0), stop=(cc == 7)),
                        deps=[tw0, tw1, tok_XPRE] + (b.dep_w(psnames[pb]) if cc == 0 else []))
                b.wr(psnames[pb], ymt)
                b.rd("WS0", ymt)
                b.rd("WS1", ymt)
                b.rd("XP", ymt)
                tye = b.Sc(lambda e, co=co, pb=pb: e.copy(YB[:, co * T:(co + 1) * T], psums[pb][:, :]),
                           deps=[ymt] + b.dep_w("YB#" + str(co)))
                b.wr("YB#" + str(co), tye)
                b.rd(psnames[pb], tye)
            tok_YB = (b.ss, b.ss.n)
            b.wr("YB", tok_YB)

            if debug and tau == 0:
                tt_ = b.DX(lambda e: e.dma_start(out=dbg["d_y"], in_=YB[:, :]), deps=[tok_YB])
                b.rd("YB", tt_)

            def half_views(h):
                lo = h * 32
                Lh = KB2[:, lo:lo + 32].rearrange("p (g r c) -> p g r c", r=4, c=4)
                Eh = KT[:, lo:lo + 32].rearrange("p (g r c) -> p g r c", r=4, c=4)
                return Lh, Eh
            Lh = [half_views(0)[0], half_views(1)[0]]
            Eh = [half_views(0)[1], half_views(1)[1]]
            SSh = [SS2[:, 0:8], SS2[:, 8:16]]
            ASh = [AS[:, 0:8], AS[:, 8:16]]
            tok_prev = [tz, tz]
            tok_kt = [b.tok.get("KT"), b.tok.get("KT")]

            def half_step(h, axis, first):
                Lv_ = Lh[h] if axis == "c" else Lh[h].transpose([0, 1, 3, 2])
                Ev_ = Eh[h] if axis == "c" else Eh[h].transpose([0, 1, 3, 2])
                lo = h * 32
                if first:
                    t1 = b.V(lambda e: e.tensor_reduce(SSh[h], Lv_, AX.X, ALU.max, negate=True),
                             deps=[tok_prev[h]])
                    t2 = b.V(lambda e: e.tensor_tensor(
                        Lv_, Lv_, SSh[h].rearrange("p (g a) -> p g a", a=4).unsqueeze(3).broadcast_to([128, 2, 4, 4]),
                        ALU.add), deps=[t1])
                else:
                    t2 = tok_prev[h]
                t3 = b.Sc(lambda e: e.activation(KT[:, lo:lo + 32], KB2[:, lo:lo + 32], AF.Exp),
                          deps=[t2, tok_kt[h]])
                tok_kt[h] = t3
                t4 = b.V(lambda e: e.tensor_reduce(SSh[h], Ev_, AX.X, ALU.add),
                         deps=[t3, t2])
                t4b = b.V(lambda e: e.tensor_single_scalar(SSh[h], SSh[h], 1e-35, ALU.max),
                          deps=[t4])
                t5 = b.Sc(lambda e: e.activation(ASh[h], SSh[h], AF.Ln), deps=[t4b])
                t6 = b.V(lambda e: e.tensor_sub(
                    Lv_, Lv_, ASh[h].rearrange("p (g a) -> p g a", a=4).unsqueeze(3).broadcast_to([128, 2, 4, 4])),
                    deps=[t5, t2])
                tok_prev[h] = t6
                return t6

            for it in range(SINK_ITERS):
                for axis in ("c", "r"):
                    half_step(0, axis, first=(it == 0 and axis == "c"))
                    half_step(1, axis, first=(it == 0 and axis == "c"))
            tK = b.Sc(lambda e: e.activation(KT[:, :], KB2[:, :], AF.Exp),
                      deps=[tok_prev[0], tok_prev[1], tok_kt[0], tok_kt[1]] + b.dep_w("KT"))
            b.wr("KT", tK)
            lastA, lastB = tK, tK
            lastA, lastB = tok_prev[0], tok_prev[1]
            th2 = tK
            tg1 = b.P(lambda e: e.matmul(P_BC[:, 0:1], IDENT[0:32, 0:1].broadcast_to([32, 128]),
                                          GATE[0:32, 0:1], start=True, stop=True),
                      deps=b.dep_r("GATE") + b.dep_w("P_BC") + b.dep_r("IDENT"))
            b.wr("P_BC", tg1)
            tg2 = b.Sc(lambda e: e.copy(GS[:, :], P_BC[:, 0:1]), deps=[tg1] + b.dep_w("GS"))
            b.wr("GS", tg2)
            b.rd("P_BC", tg2)
            tg3 = b.V(lambda e: e.tensor_scalar(GI[:, :], GS[:, :], -1.0, 1.0, ALU.mult, ALU.add),
                      deps=[tg2] + b.dep_w("GI"))
            b.wr("GI", tg3)
            th3 = b.V(lambda e: e.tensor_scalar(HR[:, :], KT[:, :], GS[:, :], None, ALU.mult),
                      deps=[th2, tg2] + b.dep_w("HR"))
            b.wr("HR", th3)
            b.rd("KT", th3)
            th4 = b.V(lambda e: e.scalar_tensor_tensor(HR[:, :], EYE[:, :], GI[:, :], HR[:, :],
                                                       ALU.mult, ALU.add),
                      deps=[th3, tg3] + b.dep_r("EYE"))
            b.wr("HR", th4)


            ttr3 = None
            for g in range(4):
                ttr3 = b.P(lambda e, g=g: e.transpose(
                    P_MM[0:16, g * 128:(g + 1) * 128], HR[:, g * 16:(g + 1) * 16], IDENT[:, :]),
                    deps=([th4] + b.dep_r("IDENT") + b.dep_w("P_MM")) if g == 0 else [th4])
            b.wr("P_MM", ttr3)
            b.rd("HR", ttr3)
            tfr = b.V(lambda e: e.tensor_copy(HFMR[0:16, :], P_MM[0:16, 0:T]),
                      deps=[ttr3] + b.dep_w("HFMR"))
            b.wr("HFMR", tfr)
            b.rd("P_MM", tfr)

            if debug and tau == 0:
                tt_ = b.DX(lambda e: e.dma_start(out=dbg["d_hfm"], in_=HFMR[0:16, :]), deps=[tfr])
                b.rd("HFMR", tt_)

            YB8 = YB[:, :].rearrange("p (cc t) -> p cc t", t=T)
            steps = []
            for i in range(4):
                steps.append(("post", i))
                for j in range(4):
                    steps.append(("res", i, j))

            def issue_bc(k, buf):
                st_ = steps[k]
                if st_[0] == "post":
                    return sel_bcast(HFMPP, "HFMPP", 4 + st_[1], [tfp], buf)
                return sel_bcast(HFMR, "HFMR", st_[1] * 4 + st_[2], [tfr], buf)

            tbb2 = [None, None]
            tbb2[0] = issue_bc(0, 0)
            for k, st_ in enumerate(steps):
                buf = k % 2
                if k + 1 < len(steps):
                    tbb2[(k + 1) % 2] = issue_bc(k + 1, (k + 1) % 2)
                dstn = BCBUF[buf][1]
                i = st_[1]
                OV8 = O[:, i * 8 * T:(i + 1) * 8 * T].rearrange("p (cc t) -> p cc t", t=T)
                if st_[0] == "post":
                    tt = b.V(lambda e, OV8=OV8, buf=buf: e.tensor_tensor(OV8, YB8, bc8(buf), ALU.mult),
                             deps=[tbb2[buf], tok_YB] + b.dep_w("XP") + b.dep_w("FO#" + str(i)))
                    b.wr("FO#" + str(i), tt)
                    b.rd("YB", tt)
                    b.rd(dstn, tt)
                else:
                    j = st_[2]
                    t_1 = b.V(lambda e, j=j, buf=buf: e.tensor_tensor(TM8v, XN8(j), bc8(buf), ALU.mult),
                              deps=[tbb2[buf]] + b.dep_r("XN") + b.dep_w("TM8"))
                    b.wr("TM8", t_1)
                    b.rd("XN", t_1)
                    b.rd(dstn, t_1)
                    tt = b.V(lambda e, OV8=OV8: e.tensor_tensor(OV8, OV8, TM8v, ALU.add),
                             deps=[t_1, b.tok["FO#" + str(i)]])
                    b.wr("FO#" + str(i), tt)
                    b.rd("TM8", tt)
            tok_FO = (b.sv, b.sv.n)
            b.wr("O", tok_FO)

            r1 = r2 = None
            for dc in range(NCH):
                pb = dc % 2
                SQ = SQ2b[pb]
                sqn = "SQ" + str(pb)
                tdone = b.tok["FO#" + str(dc // 8)]
                tq = b.Sc(lambda e, dc=dc, SQ=SQ: e.activation(SQ[:, :], Ov[:, dc, :], AF.Square),
                          deps=[tdone] + b.dep_w(sqn))
                b.wr(sqn, tq)
                t1s = b.P(lambda e, dc=dc: e.matmul(
                    P_R1[:, :], ONESB[:, :], Ov[:, dc, :], start=(dc == 0), stop=(dc == NCH - 1)),
                    deps=[tdone] + (b.dep_w("P_R1") if dc == 0 else []))
                b.rd("FO#" + str(dc // 8), t1s)
                t2s = b.P(lambda e, dc=dc, SQ=SQ: e.matmul(
                    P_R2[:, :], ONESB[:, :], SQ[:, :], start=(dc == 0), stop=(dc == NCH - 1)),
                    deps=[tq] + (b.dep_w("P_R2") if dc == 0 else []))
                b.rd(sqn, t2s)
                if dc == NCH - 1:
                    r1, r2 = t1s, t2s
            b.wr("P_R1", r1)
            b.wr("P_R2", r2)

            if debug and tau == 0:
                tt_ = b.DX(lambda e: e.dma_start(out=dbg["d_ffn"], in_=O[:, :NCH * T]), deps=[tok_FO, r1, r2])
                b.rd("O", tt_)

            ta, tb, _ = ln_finalize(r1, r2)
            ln_apply("X", X[:, :], O[:, 0:NCH * T], "O", ta, tb, resid=True)
            outv = outT.rearrange("(dc p) t -> p dc t", p=128)
            for qq in range(4):
                ts_ = b.DS(lambda e, qq=qq, tau=tau: e.dma_start(
                    out=outv[:, qq * 8:(qq + 1) * 8, tau * T:tau * T + T],
                    in_=Xv[:, qq * 8:(qq + 1) * 8, :]),
                    deps=b.dep_r("X") + [b.tok.get("X")])
                b.rd("X", ts_)

        b.emit("gpsimd", lambda e: e.nop(), waits=[(b.dS, b.dS.n), (b.dX, b.dX.n)])

        b.realize(block)
    return nc



def _rope_cache_np(Spos, Dh):
    inv_freq = (1.0 / 10000.0 ** (np.arange(0, Dh, 2, dtype=np.float32) / Dh)).astype(np.float32)
    t = np.arange(Spos, dtype=np.float32)
    f = np.outer(t, inv_freq).astype(np.float32)
    cos = np.zeros((Spos, Dh), dtype=np.float32)
    sin = np.zeros((Spos, Dh), dtype=np.float32)
    cos[:, 0::2] = np.cos(f)
    cos[:, 1::2] = np.cos(f)
    sin[:, 0::2] = np.sin(f)
    sin[:, 1::2] = np.sin(f)
    return cos, sin


def prep_inputs(L0_attn_x, L0_attn_Wq, L0_attn_Wk, L0_attn_Wv, L0_attn_Wo,
                L0_ffn_W_res, L0_ffn_W_pre, L0_ffn_W_post, L0_ffn_H_res_base,
                L0_ffn_H_pre_base, L0_ffn_H_post_base, L0_ffn_gate, L0_ffn_W_branch):
    x = np.asarray(L0_attn_x, dtype=np.float32)[0]
    Wq = np.asarray(L0_attn_Wq, dtype=np.float32)
    Wk = np.asarray(L0_attn_Wk, dtype=np.float32)
    Wv = np.asarray(L0_attn_Wv, dtype=np.float32)
    Wo = np.asarray(L0_attn_Wo, dtype=np.float32)
    Wres = np.asarray(L0_ffn_W_res, dtype=np.float32)
    Wpre = np.asarray(L0_ffn_W_pre, dtype=np.float32)
    Wpost = np.asarray(L0_ffn_W_post, dtype=np.float32)
    Hres_b = np.asarray(L0_ffn_H_res_base, dtype=np.float32)
    Hpre_b = np.asarray(L0_ffn_H_pre_base, dtype=np.float32)
    Hpost_b = np.asarray(L0_ffn_H_post_base, dtype=np.float32)
    gate = np.asarray(L0_ffn_gate, dtype=np.float32)
    Wb = np.asarray(L0_ffn_W_branch, dtype=np.float32)

    wq_h = np.ascontiguousarray(Wq.reshape(NH, 128, NCH, 128).transpose(0, 3, 2, 1)).reshape(NH, 128, D)
    wo_h = np.ascontiguousarray(Wo.reshape(NH, 128, NCH, 128).transpose(0, 3, 2, 1)).reshape(NH, 128, D)
    wk_h = np.ascontiguousarray(Wk.T.reshape(NCH, 128, 128).transpose(1, 0, 2)).reshape(128, D)
    wv_h = np.ascontiguousarray(Wv.T.reshape(NCH, 128, 128).transpose(1, 0, 2)).reshape(128, D)
    wb_h = np.ascontiguousarray(Wb.reshape(8, 128, 1024).transpose(1, 0, 2)).reshape(128, 8 * 1024)
    Wcat = np.concatenate([Wres, Wpre, Wpost], axis=0)
    wlog_h = np.ascontiguousarray(Wcat.T.reshape(NCH, 128, 24).transpose(1, 0, 2)).reshape(128, NCH * 24)
    bases = np.concatenate([Hres_b.reshape(-1), Hpre_b.reshape(-1), Hpost_b.reshape(-1)]).reshape(24, 1)
    bases = np.ascontiguousarray(bases, dtype=np.float32)
    gate_h = gate.reshape(1, 1).astype(np.float32)
    eye_h = np.ascontiguousarray(np.broadcast_to(np.tile(np.eye(4, dtype=np.float32).reshape(-1), 4), (128, 64)))
    ident_h = np.eye(128, dtype=np.float32)

    cos_full, sin_full = _rope_cache_np(S, DH)
    lin = np.linspace(0.0, 1.0, 256, dtype=np.float32)

    in_maps = []
    for c in range(NCORES):
        lo = 2048 * c - 256
        hi = 2048 * c + 2304
        xTc = np.zeros((D, TEXT), dtype=np.float32)
        a = max(lo, 0)
        bnd = min(hi, S)
        xTc[:, a - lo:bnd - lo] = x[a:bnd].T
        pos = np.clip(np.arange(lo, hi), 0, S - 1)
        trigc_h = np.ascontiguousarray(cos_full[pos].T).astype(ml_dtypes.bfloat16)
        sgn = sin_full[pos].T.copy()
        sgn[0:64, :] *= -1.0
        trigs_h = np.ascontiguousarray(sgn).astype(ml_dtypes.bfloat16)
        alph_h = np.zeros((2 * NT, 512), dtype=np.float32)
        for k in range(2 * NT):
            if c == 0 and k == 0:
                alph_h[k, 0:256] = 0.0
                alph_h[k, 256:512] = 1.0
            else:
                alph_h[k, 0:256] = 1.0 - lin
                alph_h[k, 256:512] = lin
        corr_h = np.zeros((1, 2 * NT), dtype=np.float32)
        if c == NCORES - 1:
            corr_h[0, 2 * NT - 1] = 128.0
        in_maps.append({
            "xT": np.ascontiguousarray(xTc),
            "wq": wq_h, "wo": wo_h, "wk": wk_h, "wv": wv_h, "wb": wb_h, "wlog": wlog_h,
            "trigc": trigc_h, "trigs": trigs_h, "alph": alph_h, "corrb": corr_h,
            "basesI": bases, "gateI": gate_h, "eyeI": eye_h, "identI": ident_h,
        })
    return in_maps


def kernel(**inputs):
    in_maps = prep_inputs(**inputs)
    nc = build_program()
    res = run_bass_kernel_spmd(nc, in_maps, core_ids=list(range(NCORES)))
    out = np.empty((1, S, D), dtype=np.float32)
    for c in range(NCORES):
        out[0, 2048 * c:2048 * (c + 1), :] = res.results[c]["outT"].T
    return out
